# version 06
# baseline (speedup 1.0000x reference)
import numpy as np
import concourse.bass as bass
import concourse.mybir as mybir
from concourse.bass_utils import run_bass_kernel_spmd

F32 = mybir.dt.float32
BF16 = mybir.dt.bfloat16
AF = mybir.ActivationFunctionType
ALU = mybir.AluOpType
AX = mybir.AxisListType

B, S, D = 4, 4096, 1024
SK = S + 1
QR = 2048
NT1 = 32
NG1 = 8
NG2 = 4
EPS = 1e-5


class Ch:
    __slots__ = ("name", "n", "sem")
    def __init__(self, name):
        self.name, self.n, self.sem = name, 0, None


class Ev:
    __slots__ = ("eng", "fn", "waits", "inc", "tag")
    def __init__(self, eng, fn, waits, inc, tag):
        self.eng, self.fn, self.waits, self.inc, self.tag = eng, fn, waits, inc, tag


class Prog:
    def __init__(self):
        self.events = []

    def emit(self, eng, fn, waits=(), inc=None, tag=""):
        waits = [w for w in waits if w is not None]
        tok = None
        if inc is not None:
            if isinstance(inc, Ch):
                inc = (inc, 16 if inc.name.startswith("d") else 1)
            ch, k = inc
            ch.n += k
            tok = (ch, ch.n)
        self.events.append(Ev(eng, fn, waits, inc, tag))
        return tok

    def check(self):
        import collections
        q = collections.defaultdict(collections.deque)
        for ev in self.events:
            q[ev.eng].append(ev)
        vals = collections.defaultdict(int)
        progress = True
        while progress:
            progress = False
            for eng in list(q):
                while q[eng]:
                    ev = q[eng][0]
                    if all(vals[c.name] >= v for c, v in ev.waits):
                        q[eng].popleft()
                        if ev.inc is not None:
                            vals[ev.inc[0].name] += ev.inc[1]
                        progress = True
                    else:
                        break
        stuck = {e: (d[0].tag, [(c.name, v, vals[c.name]) for c, v in d[0].waits])
                 for e, d in q.items() if d}
        if stuck:
            raise RuntimeError(f"schedule deadlock: {stuck}")

    def realize(self, block):
        import collections
        marks = collections.defaultdict(int)

        def body(eng_name):
            def run(eng):
                for ev in self.events:
                    if ev.eng != eng_name:
                        continue
                    for ch, v in ev.waits:
                        if marks[(eng_name, ch.name)] < v:
                            eng.wait_ge(ch.sem, v)
                            marks[(eng_name, ch.name)] = v
                    inst = ev.fn(eng)
                    if ev.inc is not None:
                        inst.then_inc(ev.inc[0].sem, ev.inc[1])
            return run
        block.sync(body("sync"))
        block.gpsimd(body("gpsimd"))
        block.tensor(body("tensor"))
        block.vector(body("vector"))
        block.scalar(body("scalar"))


def build_nc():
    import contextlib
    import ml_dtypes
    nc = bass.Bass("TRN2", target_bir_lowering=False)

    x_h = nc.dram_tensor("x", [S, D], F32, kind="ExternalInput").ap()
    wq_h = nc.dram_tensor("wq", [129, D], F32, kind="ExternalInput").ap()
    bq_h = nc.dram_tensor("bq", [129], F32, kind="ExternalInput").ap()
    wk_h = nc.dram_tensor("wk", [129, D], F32, kind="ExternalInput").ap()
    bk_h = nc.dram_tensor("bk", [129], F32, kind="ExternalInput").ap()
    wv_h = nc.dram_tensor("wv", [D, D], F32, kind="ExternalInput").ap()
    bv_h = nc.dram_tensor("bv", [D], F32, kind="ExternalInput").ap()
    k0_h = nc.dram_tensor("k0", [129], F32, kind="ExternalInput").ap()
    v0_h = nc.dram_tensor("v0", [D], F32, kind="ExternalInput").ap()
    U1_h = nc.dram_tensor("U1", [D, 256], F32, kind="ExternalInput").ap()
    V1_h = nc.dram_tensor("V1", [256, 4096], F32, kind="ExternalInput").ap()
    b1_h = nc.dram_tensor("b1", [4096], F32, kind="ExternalInput").ap()
    U2_h = nc.dram_tensor("U2", [4096, 256], F32, kind="ExternalInput").ap()
    V2_h = nc.dram_tensor("V2", [256, D], F32, kind="ExternalInput").ap()
    b2_h = nc.dram_tensor("b2", [D], F32, kind="ExternalInput").ap()
    out_h = nc.dram_tensor("out", [QR, D], F32, kind="ExternalOutput").ap()
    ident_h = nc.inline_tensor(np.eye(128, dtype=ml_dtypes.bfloat16), "identity")

    ctx = contextlib.ExitStack()
    sb = lambda name, shape, dt: ctx.enter_context(nc.sbuf_tensor(name, shape, dt))
    psu = lambda name, shape, dt: ctx.enter_context(nc.psum_tensor(name, shape, dt))

    with ctx:
        XN = sb("XN", [128, NT1 * D], BF16)
        K_hi = sb("K_hi", [128, SK], BF16)
        K_lo = sb("K_lo", [1, SK], BF16)
        wqT = sb("wqT", [128, 8 * 129], BF16)
        wkT = sb("wkT", [128, 8 * 129], BF16)
        wvT = sb("wvT", [128, 8 * D], BF16)
        U1s = sb("U1s", [128, 8 * 256], BF16)
        V1s = sb("V1s", [128, 2 * 4096], BF16)
        U2s = sb("U2s", [128, 32 * 256], BF16)
        V2s = sb("V2s", [128, 2 * D], BF16)
        bvv0 = sb("bvv0", [2, D], BF16)
        b2row = sb("b2row", [1, D], BF16)
        b1T = sb("b1T", [128, 32], F32)
        bqT = sb("bqT", [128, 1], F32)
        bqL = sb("bqL", [1, 1], F32)
        bkT = sb("bkT", [128, 1], F32)
        bkL = sb("bkL", [1, 1], F32)
        ident = sb("ident", [128, 128], BF16)
        ones11 = sb("ones11", [1, 1], BF16)
        onescol = sb("onescol", [1, 128], BF16)
        onesc1 = sb("onesc1", [128, 1], BF16)
        scr = sb("scr", [128, 4100], F32)
        P_sb = sb("P_sb", [128, SK], BF16)
        PT = sb("PT", [128, 2048], BF16)
        xnT = sb("xnT", [128, 8 * 512], BF16)
        acc = sb("acc", [128, 4 * D], F32)
        xb = [sb("xb0", [128, D], F32), sb("xb1", [128, D], F32)]
        q_sb = sb("q_sb", [128, 128], BF16)
        qlo = sb("qlo", [1, 128], BF16)
        cs_bf = sb("cs_bf", [128, 2], BF16)
        csT = sb("csT", [2, 128], BF16)
        sum1 = sb("sum1", [128, 2], F32)
        sumsq = sb("sumsq", [128, 2], F32)
        mean = sb("mean", [128, 2], F32)
        msq = sb("msq", [128, 2], F32)
        m2v = sb("m2v", [128, 2], F32)
        varv = sb("varv", [128, 2], F32)
        lnv = sb("lnv", [128, 2], F32)
        rstd = sb("rstd", [128, 2], F32)
        nmrs = sb("nmrs", [128, 2], F32)
        maxp = sb("maxp", [128, 10], F32)
        negm = sb("negm", [128, 1], F32)
        Zf = sb("Zf", [128, 1], F32)
        rz = sb("rz", [128, 1], F32)
        spacer = sb("spacer", [128, 64], F32)
        sspace = sb("sspace", [128, 1], F32)
        k0f = sb("k0f", [128, 1], F32)
        k0l = sb("k0l", [1, 1], F32)
        eps_c = sb("eps_c", [128, 1], F32)
        nhalf_c = sb("nhalf_c", [128, 1], F32)

        xn_f32 = scr[:, 0:1024]
        Pf = scr[:, 1024:1536].bitcast(BF16)
        H2 = scr[:, 1536:2048].bitcast(BF16)
        Qf = scr[:, 2048:2560].bitcast(BF16)
        junkS = scr[:, 2560:3072].bitcast(BF16)
        junkV = scr[:, 3072:3584].bitcast(BF16)
        U_bf = P_sb[:, 0:1024]
        UT = P_sb[:, 1024:2048]
        lo_bf = P_sb[0:1, 2048:3072]
        hi_bf = P_sb[:, 0:1024]

        pA = psu("pA", [128, 1024], F32)
        pB = psu("pB", [128, 1024], F32)
        pC = psu("pC", [128, 1024], F32)
        pD = psu("pD", [128, 1024], F32)
        pB_bf = pB[:].bitcast(BF16)
        pC_bf = pC[:].bitcast(BF16)

        P = Prog()
        dX = [Ch("dX0"), Ch("dX1")]
        dW = [Ch("dW0"), Ch("dW1")]
        dWs, dO = Ch("dWs"), Ch("dO")
        V, SC, T = Ch("V"), Ch("S"), Ch("T")
        for c in (dX[0], dX[1], dW[0], dW[1], dWs, dO, V, SC, T):
            c.sem = ctx.enter_context(nc.semaphore(c.name))

        bank_free = {i: None for i in range(8)}
        em = P.emit

        def em2dma(eng, fn, waits=(), inc=None, tag=""):
            em(eng, fn, waits=waits, inc=inc, tag=tag)
            return em(eng, fn, inc=inc, tag=tag + "2")

        def vspace(tag="vsp"):
            em("vector", lambda e: e.memset(spacer[:], 0.0), tag=tag)

        for _ in range(1500):
            em("gpsimd", lambda e: e.nop(cycle_cnt=4000), tag="delay")
        for _ in range(1500):
            em("sync", lambda e: e.nop(cycle_cnt=4000), tag="delay")
        em("gpsimd", lambda e: e.dma_start(out=xb[0][:, 0:512], in_=x_h[0:128, 0:512]),
           inc=dW[0], tag="boot")
        em("gpsimd", lambda e: e.dma_start(out=xb[1][:, 0:512], in_=x_h[0:128, 0:512]),
           inc=dW[1], tag="boot")
        em("gpsimd", lambda e: e.dma_start(out=spacer[:], in_=x_h[0:128, 0:64]),
           inc=dWs, tag="boot")
        em("sync", lambda e: e.dma_start(out=xb[0][:, 512:1024], in_=x_h[0:128, 512:1024]),
           inc=dX[0], tag="boot")
        em("sync", lambda e: e.dma_start(out=xb[1][:, 512:1024], in_=x_h[0:128, 512:1024]),
           inc=dX[1], tag="boot")
        t_bw = [(dW[0], dW[0].n), (dW[1], dW[1].n), (dWs, dWs.n),
                (dX[0], dX[0].n), (dX[1], dX[1].n)]
        em("vector", lambda e: e.reduce_sum(sum1[:, 0:1], xb[0][:], axis=AX.X),
           waits=t_bw, inc=V, tag="bootrd")
        em("vector", lambda e: e.reduce_sum(sum1[:, 1:2], xb[1][:], axis=AX.X),
           inc=V, tag="bootrd")
        t_boot = (V, V.n)

        em2dma("gpsimd", lambda e: e.dma_start(out=ident[:], in_=ident_h.ap()),
           waits=[t_boot], inc=dWs)
        em2dma("gpsimd", lambda e: e.dma_start(
            out=bqT[:], in_=bq_h[0:128].rearrange("(p o) -> p o", o=1),
            allow_slow_non_contiguous=True), inc=dWs)
        em2dma("gpsimd", lambda e: e.dma_start(
            out=bqL[:], in_=bq_h[128:129].rearrange("(p o) -> p o", o=1)), inc=dWs)
        em2dma("gpsimd", lambda e: e.dma_start(
            out=bkT[:], in_=bk_h[0:128].rearrange("(p o) -> p o", o=1),
            allow_slow_non_contiguous=True), inc=dWs)
        em2dma("gpsimd", lambda e: e.dma_start(
            out=bkL[:], in_=bk_h[128:129].rearrange("(p o) -> p o", o=1)), inc=dWs)
        em2dma("gpsimd", lambda e: e.dma_start(
            out=b1T[:], in_=b1_h.rearrange("(c p) -> p c", p=128),
            allow_slow_non_contiguous=True), inc=dWs)
        em2dma("gpsimd", lambda e: e.dma_start(
            out=k0f[:], in_=k0_h[0:128].rearrange("(p o) -> p o", o=1),
            allow_slow_non_contiguous=True), inc=dWs)
        em2dma("gpsimd", lambda e: e.dma_start(
            out=k0l[:], in_=k0_h[128:129].rearrange("(p o) -> p o", o=1)), inc=dWs)
        em2dma("gpsimd", lambda e: e.dma_start(
            out=b2row[0:1, :], in_=b2_h.rearrange("(o d) -> o d", o=1)), inc=dWs)
        em2dma("gpsimd", lambda e: e.dma_start(
            out=bvv0[0:1, :], in_=bv_h.rearrange("(o d) -> o d", o=1)), inc=dWs)
        em2dma("gpsimd", lambda e: e.dma_start(
            out=bvv0[1:2, :], in_=v0_h.rearrange("(o d) -> o d", o=1)), inc=dWs)
        t_smalls = (dWs, dWs.n)
        t_ident = t_smalls
        t_bias = t_smalls
        t_wrow = t_smalls
        em("vector", lambda e: e.memset(ones11[:], 1.0), inc=V)
        em("vector", lambda e: e.memset(onescol[:], 1.0), inc=V)
        em("vector", lambda e: e.memset(onesc1[:], 1.0), inc=V)
        t_ones = (V, V.n)
        em("vector", lambda e: e.memset(eps_c[:], EPS), inc=V)
        em("vector", lambda e: e.memset(nhalf_c[:], -0.5), inc=V)
        t_consts = (V, V.n)
        em("vector", lambda e: e.tensor_copy(K_hi[:, 4096:4097], k0f[:]),
           waits=[t_smalls], inc=V)
        em("vector", lambda e: e.tensor_copy(K_lo[0:1, 4096:4097], k0l[:]), inc=V)
        t_k0 = (V, V.n)

        stage_tok = [t_boot, t_boot]
        load_tok = [None, None]
        stage_i = [0]

        def stage_load(src_ap, rows=128, cols=D):
            i = stage_i[0] & 1
            stage_i[0] += 1
            dst = xb[i][0:rows, 0:cols]
            tk = em2dma("gpsimd", lambda e, d=dst, s=src_ap: e.dma_start(out=d, in_=s),
                    waits=[stage_tok[i]], inc=dW[i], tag="wload")
            load_tok[i] = tk
            return i, dst

        def load_cast(src_ap, dst_bf, rows=128, cols=D, extra=None):
            i, st = stage_load(src_ap, rows, cols)
            tk = em("vector", lambda e, d=dst_bf, s=st: e.tensor_copy(d, s),
                    waits=[load_tok[i], extra], inc=V, tag="wcast")
            stage_tok[i] = tk
            return tk

        for c in range(8):
            load_cast(U1_h[c * 128:(c + 1) * 128, :], U1s[:, c * 256:(c + 1) * 256], cols=256)
        for m in range(2):
            for pce in range(4):
                load_cast(V1_h[m * 128:(m + 1) * 128, pce * 1024:(pce + 1) * 1024],
                          V1s[:, m * 4096 + pce * 1024: m * 4096 + (pce + 1) * 1024])
        for c in range(32):
            load_cast(U2_h[c * 128:(c + 1) * 128, :], U2s[:, c * 256:(c + 1) * 256], cols=256)
        for m in range(2):
            load_cast(V2_h[m * 128:(m + 1) * 128, :], V2s[:, m * D:(m + 1) * D])

        def prep_qk(w_h, wT):
            tcast = load_cast(w_h[0:128, :], hi_bf, extra=bank_free[4])
            for c in range(8):
                em("tensor", lambda e, c=c: e.transpose(
                    pC_bf[:, c * 128:(c + 1) * 128], hi_bf[:, c * 128:(c + 1) * 128], ident[:]),
                    waits=[tcast, t_ident, bank_free[4] if c == 0 else None],
                    inc=(T, 1) if c == 7 else None, tag="qkT")
            t_tr = (T, T.n)
            wT3 = wT[:].rearrange("p (c k) -> p c k", k=129)
            pc3 = pC_bf[:, 0:1024].rearrange("p (c k) -> p c k", k=128)
            ev = em("vector", lambda e, w3=wT3, p3=pc3: e.tensor_copy(w3[:, :, 0:128], p3),
                    waits=[t_tr], inc=V, tag="qkTev")
            bank_free[4] = ev
            tc2 = load_cast(w_h[128:129, :], lo_bf[0:1, :], rows=1, extra=bank_free[6])
            for c in range(8):
                em("tensor", lambda e, c=c: e.matmul(
                    pD[:, c:c + 1], lo_bf[0:1, c * 128:(c + 1) * 128], ones11[:],
                    start=True, stop=True),
                    waits=[tc2, t_ones, bank_free[6] if c == 0 else None],
                    inc=(T, 1) if c == 7 else None, tag="qklo")
            t_lo = (T, T.n)
            ev2 = em("vector", lambda e, w3=wT3: e.tensor_copy(
                w3[:, :, 128:129], pD[:, 0:8].rearrange("p (c k) -> p c k", k=1)),
                waits=[t_lo], inc=V, tag="qkloev")
            bank_free[6] = ev2

        prep_qk(wq_h, wqT)
        prep_qk(wk_h, wkT)

        wv3 = wvT[:].rearrange("p (dc e) -> p dc e", e=1024)
        for ec in range(8):
            tcast = load_cast(wv_h[ec * 128:(ec + 1) * 128, :], hi_bf, extra=bank_free[4])
            for c in range(8):
                em("tensor", lambda e, c=c: e.transpose(
                    pC_bf[:, c * 128:(c + 1) * 128], hi_bf[:, c * 128:(c + 1) * 128], ident[:]),
                    waits=[tcast, bank_free[4] if c == 0 else None],
                    inc=(T, 1) if c == 7 else None, tag="wvT")
            t_tr = (T, T.n)
            ev = em("vector", lambda e, ec=ec: e.tensor_copy(
                wv3[:, :, ec * 128:(ec + 1) * 128],
                pC_bf[:, 0:1024].rearrange("p (c k) -> p c k", k=128)),
                waits=[t_tr], inc=V, tag="wvTev")
            bank_free[4] = ev

        chain_tok = [None, None]

        def ln_stats(x_t, t_x, p, extra_v=None):
            em("vector", lambda e, p=p: e.scalar_tensor_tensor(
                junkV, x_t, 1.0, x_t, op0=ALU.mult, op1=ALU.mult,
                accum_out=sumsq[:, p:p + 1]),
                waits=[t_x, extra_v], inc=V, tag="lnsq")
            em("vector", lambda e, p=p: e.reduce_sum(
                sum1[:, p:p + 1], x_t, axis=AX.X), tag="lnsum")
            return None

        def ln_chain(x_t, p, t_stats, out_ap):
            em("vector", lambda e, p=p: e.tensor_scalar_mul(
                mean[:, p:p + 1], sum1[:, p:p + 1], 1.0 / D), tag="lnmean")
            em("vector", lambda e, p=p: e.tensor_scalar_mul(
                msq[:, p:p + 1], sumsq[:, p:p + 1], 1.0 / D), tag="lnmsq")
            em("vector", lambda e, p=p: e.tensor_mul(
                m2v[:, p:p + 1], mean[:, p:p + 1], mean[:, p:p + 1]), tag="lnm2")
            vspace()
            tv2 = em("vector", lambda e, p=p: e.tensor_sub(
                varv[:, p:p + 1], msq[:, p:p + 1], m2v[:, p:p + 1]), inc=V, tag="lnvar")
            em("scalar", lambda e, p=p: e.activation(
                lnv[:, p:p + 1], varv[:, p:p + 1], AF.Ln, bias=eps_c[:]),
                waits=[tv2, t_consts], tag="lnln")
            tr = em("scalar", lambda e, p=p: e.activation(
                rstd[:, p:p + 1], lnv[:, p:p + 1], AF.Exp, scale=nhalf_c[:]),
                inc=SC, tag="lnexp")
            tn = em("vector", lambda e, p=p: e.scalar_tensor_tensor(
                nmrs[:, p:p + 1], mean[:, p:p + 1], -1.0, rstd[:, p:p + 1],
                op0=ALU.mult, op1=ALU.mult),
                waits=[tr], inc=V, tag="lnnm")
            chain_tok[p] = tn
            ta = em("scalar", lambda e, o=out_ap, p=p: e.activation(
                o, x_t, AF.Identity, bias=nmrs[:, p:p + 1], scale=rstd[:, p:p + 1]),
                waits=[tn], inc=SC, tag="lnaff")
            return ta

        def xnt_group(tiles, src_toks):
            for c in range(8):
                bk = 4 + (c & 1)
                off = (c & 1) * 1024
                for j, tl in enumerate(tiles):
                    em("tensor", lambda e, c=c, tl=tl, off=off, j=j: e.transpose(
                        pC_bf[:, off + j * 128: off + (j + 1) * 128],
                        XN[:, tl * D + c * 128: tl * D + (c + 1) * 128], ident[:]),
                        waits=[src_toks[j], bank_free[bk] if j == 0 else None],
                        inc=(T, 1) if j == 3 else None, tag="xnT")
                t_tr = (T, T.n)
                ev = em("vector", lambda e, c=c, off=off: e.tensor_copy(
                    xnT[:, c * 512:(c + 1) * 512], pC_bf[:, off:off + 512]),
                    waits=[t_tr], inc=V, tag="xnTev")
                bank_free[bk] = ev
            return (V, V.n)

        xb_tok = [stage_tok[0], stage_tok[1]]
        xn_tok = [None] * NT1
        for g in range(NG1):
            stats_tok = [None] * 4
            for t in range(4):
                tile = g * 4 + t
                i = tile & 1
                tx = em2dma("sync", lambda e, tile=tile, i=i: e.dma_start(
                    out=xb[i][:], in_=x_h[tile * 128:(tile + 1) * 128, :]),
                    waits=[xb_tok[i]], inc=dX[i], tag="xload1")
                stats_tok[t] = ln_stats(xb[i][:], tx, i)
                if t > 0:
                    tile_p = g * 4 + t - 1
                    pt = tile_p & 1
                    ta = ln_chain(xb[pt][:], pt, stats_tok[t - 1],
                                  XN[:, tile_p * D:(tile_p + 1) * D])
                    xb_tok[pt] = ta
                    xn_tok[tile_p] = ta
            tile_p = g * 4 + 3
            ta = ln_chain(xb[tile_p & 1][:], tile_p & 1, stats_tok[3],
                          XN[:, tile_p * D:(tile_p + 1) * D])
            xb_tok[tile_p & 1] = ta
            xn_tok[tile_p] = ta
            t_xnt = xnt_group([g * 4 + t for t in range(4)],
                              [xn_tok[g * 4 + t] for t in range(4)])
            for c in range(8):
                em("tensor", lambda e, c=c: e.matmul(
                    pA[:, 0:512], wkT[:, c * 129:c * 129 + 128], xnT[:, c * 512:(c + 1) * 512],
                    start=(c == 0), stop=(c == 7)),
                    waits=[t_xnt, bank_free[0] if c == 0 else None], tag="kprojhi")
                em("tensor", lambda e, c=c: e.matmul(
                    pA[0:1, 512:1024], wkT[:, c * 129 + 128:c * 129 + 129],
                    xnT[:, c * 512:(c + 1) * 512],
                    start=(c == 0), stop=(c == 7)),
                    waits=[bank_free[1] if c == 0 else None],
                    inc=(T, 1) if c == 7 else None, tag="kprojlo")
            t_k = (T, T.n)
            e1 = em("scalar", lambda e, g=g: e.activation(
                K_hi[:, g * 512:(g + 1) * 512], pA[:, 0:512], AF.Identity, bias=bkT[:]),
                waits=[t_k, t_bias], inc=SC, tag="kevhi")
            e2 = em("scalar", lambda e, g=g: e.activation(
                K_lo[0:1, g * 512:(g + 1) * 512], pA[0:1, 512:1024], AF.Identity, bias=bkL[:]),
                inc=SC, tag="kevlo")
            bank_free[0] = e1
            bank_free[1] = e2
        t_xn_done = xn_tok[NT1 - 1]
        t_k_done = (SC, SC.n)

        acc_free = None
        last_exp = [None]
        for g in range(NG2):
            acc_toks = [None] * 4
            stats_tok = [None] * 4

            def emit_tail(t, g=g, acc_toks=acc_toks, stats_tok=stats_tok):
                tile = g * 4 + t
                i = tile & 1
                ta = ln_chain(xb[i][:], i, stats_tok[t], xn_f32)
                xn_tok[tile] = ta
                tadd = em("vector", lambda e, t=t, i=i: e.tensor_add(
                    acc[:, t * D:(t + 1) * D], xb[i][:], xn_f32),
                    waits=[ta, acc_free], inc=V, tag="accinit")
                xb_tok[i] = tadd
                acc_toks[t] = tadd

            for t in range(4):
                tile = g * 4 + t
                i = tile & 1
                tx = em2dma("sync", lambda e, tile=tile, i=i: e.dma_start(
                    out=xb[i][:], in_=x_h[tile * 128:(tile + 1) * 128, :]),
                    waits=[xb_tok[i]], inc=dX[i], tag="xload2")
                stats_tok[t] = ln_stats(xb[i][:], tx, i,
                                        extra_v=last_exp[0] if t == 0 else None)
                if t > 0:
                    emit_tail(t - 1)
            emit_tail(3)
            t_xnt = xnt_group([g * 4 + t for t in range(4)],
                              [xn_tok[g * 4 + t] for t in range(4)])
            for c in range(8):
                for m in range(2):
                    em("tensor", lambda e, c=c, m=m: e.matmul(
                        pA[:, m * 512:(m + 1) * 512],
                        U1s[:, c * 256 + m * 128: c * 256 + (m + 1) * 128],
                        xnT[:, c * 512:(c + 1) * 512], start=(c == 0), stop=(c == 7)),
                        waits=[t_xnt if c == 0 else None,
                               bank_free[m] if c == 0 else None],
                        inc=(T, 1) if (c == 7 and m == 1) else None, tag="ffnP")
            t_p = (T, T.n)
            ep = em("scalar", lambda e: e.activation(Pf, pA[:], AF.Copy),
                    waits=[t_p], inc=SC, tag="ffnPev")
            bank_free[0] = ep
            bank_free[1] = ep
            gelu_tok = [None, None]
            for j in range(32):
                jb = j & 1
                em("tensor", lambda e, j=j, jb=jb: e.matmul(
                    pB[:, jb * 512:(jb + 1) * 512],
                    V1s[:, 0 * 4096 + j * 128:0 * 4096 + (j + 1) * 128],
                    Pf[:, 0:512], start=True, stop=False),
                    waits=[ep if j < 2 else None,
                           (gelu_tok[jb] if j >= 2 else bank_free[2 + jb])], tag="ffnH")
                em("tensor", lambda e, j=j, jb=jb: e.matmul(
                    pB[:, jb * 512:(jb + 1) * 512],
                    V1s[:, 1 * 4096 + j * 128:1 * 4096 + (j + 1) * 128],
                    Pf[:, 512:1024], start=False, stop=True),
                    inc=(T, 1), tag="ffnH2")
                t_h = (T, T.n)
                gt = em("scalar", lambda e, j=j, jb=jb: e.activation(
                    H2[:, jb * 512:(jb + 1) * 512], pB[:, jb * 512:(jb + 1) * 512],
                    AF.Gelu_apprx_tanh, bias=b1T[:, j:j + 1]),
                    waits=[t_h, t_smalls], inc=SC, tag="gelu")
                gelu_tok[jb] = gt
                for m in range(2):
                    em("tensor", lambda e, j=j, jb=jb, m=m: e.matmul(
                        pC[:, m * 512:(m + 1) * 512],
                        U2s[:, j * 256 + m * 128:j * 256 + (m + 1) * 128],
                        H2[:, jb * 512:(jb + 1) * 512], start=(j == 0), stop=(j == 31)),
                        waits=[gt, (bank_free[4 + m] if j == 0 else None)],
                        inc=(T, 1) if (j == 31 and m == 1) else None, tag="ffnQ")
            bank_free[2] = gelu_tok[0]
            bank_free[3] = gelu_tok[1]
            t_q = (T, T.n)
            eq = em("vector", lambda e: e.tensor_copy(Qf, pC[:]),
                    waits=[t_q], inc=V, tag="ffnQev")
            bank_free[4] = eq
            bank_free[5] = eq
            for t in range(4):
                for nh in range(2):
                    for m in range(2):
                        em("tensor", lambda e, t=t, nh=nh, m=m: e.matmul(
                            pD[:, nh * 512:(nh + 1) * 512],
                            Qf[:, m * 512 + t * 128: m * 512 + (t + 1) * 128],
                            V2s[:, m * D + nh * 512: m * D + (nh + 1) * 512],
                            start=(m == 0), stop=False),
                            waits=[eq if m == 0 else None,
                                   bank_free[6 + nh] if m == 0 else None], tag="ffnO")
                    em("tensor", lambda e, nh=nh: e.matmul(
                        pD[:, nh * 512:(nh + 1) * 512], onescol[:],
                        b2row[0:1, nh * 512:(nh + 1) * 512], start=False, stop=True),
                        waits=[t_ones, t_wrow],
                        inc=(T, 1) if nh == 1 else None, tag="ffnOb")
                t_o = (T, T.n)
                eo = em("vector", lambda e, t=t: e.tensor_add(
                    acc[:, t * D:(t + 1) * D], acc[:, t * D:(t + 1) * D], pD[:]),
                    waits=[t_o, acc_toks[t]], inc=V, tag="accO")
                bank_free[6] = eo
                bank_free[7] = eo
            for t in range(4):
                for c in range(8):
                    em("tensor", lambda e, c=c, t=t: e.matmul(
                        pA[:, 0:128], wqT[:, c * 129:c * 129 + 128],
                        xnT[:, c * 512 + t * 128: c * 512 + (t + 1) * 128],
                        start=(c == 0), stop=(c == 7)),
                        waits=[bank_free[0] if c == 0 else None], tag="qhi")
                    em("tensor", lambda e, c=c, t=t: e.matmul(
                        pA[0:1, 512:640], wqT[:, c * 129 + 128:c * 129 + 129],
                        xnT[:, c * 512 + t * 128: c * 512 + (t + 1) * 128],
                        start=(c == 0), stop=(c == 7)),
                        waits=[bank_free[1] if c == 0 else None],
                        inc=(T, 1) if c == 7 else None, tag="qlo")
                t_qp = (T, T.n)
                eqh = em("scalar", lambda e: e.activation(
                    q_sb[:], pA[:, 0:128], AF.Identity, bias=bqT[:]),
                    waits=[t_qp, t_bias], inc=SC, tag="qevhi")
                eql = em("scalar", lambda e: e.activation(
                    qlo[:], pA[0:1, 512:640], AF.Identity, bias=bqL[:]),
                    inc=SC, tag="qevlo")
                bank_free[0] = eqh
                slab_ev = [None] * 8
                for s_ in range(8):
                    sb_ = s_ & 1
                    em("tensor", lambda e, s_=s_, sb_=sb_: e.matmul(
                        pB[:, sb_ * 512:(sb_ + 1) * 512], q_sb[:],
                        K_hi[:, s_ * 512:(s_ + 1) * 512],
                        start=True, stop=False),
                        waits=[eqh,
                               t_k_done if s_ == 0 else None,
                               (slab_ev[s_ - 2] if s_ >= 2 else bank_free[2 + sb_])],
                        tag="schi")
                    em("tensor", lambda e, s_=s_, sb_=sb_: e.matmul(
                        pB[:, sb_ * 512:(sb_ + 1) * 512], qlo[:],
                        K_lo[0:1, s_ * 512:(s_ + 1) * 512],
                        start=False, stop=True),
                        waits=[eql], inc=(T, 1), tag="sclo")
                    t_sc = (T, T.n)
                    se = em("scalar", lambda e, s_=s_, sb_=sb_: e.activation(
                        scr[:, s_ * 512:(s_ + 1) * 512], pB[:, sb_ * 512:(sb_ + 1) * 512],
                        AF.Copy),
                        waits=[t_sc, acc_toks[3] if (t == 0 and s_ == 0) else None],
                        inc=SC, tag="scev")
                    slab_ev[s_] = se
                    em("vector", lambda e, s_=s_: e.reduce_max(
                        maxp[:, s_:s_ + 1], scr[:, s_ * 512:(s_ + 1) * 512], axis=AX.X),
                        waits=[se], tag="maxp")
                bank_free[2] = slab_ev[6]
                bank_free[3] = slab_ev[7]
                em("tensor", lambda e: e.matmul(
                    pA[:, 640:641], q_sb[:], K_hi[:, 4096:4097], start=True, stop=False),
                    waits=[eql, t_k0], tag="snkhi")
                em("tensor", lambda e: e.matmul(
                    pA[:, 640:641], qlo[:], K_lo[0:1, 4096:4097], start=False, stop=True),
                    inc=(T, 1), tag="snklo")
                t_snk = (T, T.n)
                es = em("scalar", lambda e: e.activation(
                    scr[:, 4096:4097], pA[:, 640:641], AF.Copy),
                    waits=[t_snk], inc=SC, tag="snkev")
                bank_free[1] = es
                em("vector", lambda e: e.tensor_copy(maxp[:, 8:9], scr[:, 4096:4097]),
                   waits=[es], tag="snkmax")
                vspace("vsp_negm")
                tm = em("vector", lambda e: e.reduce_max(
                    negm[:], maxp[:, 0:9], axis=AX.X, negate=True), inc=V, tag="negm")
                te = em("scalar", lambda e: e.activation(
                    P_sb[:, 0:SK], scr[:, 0:SK], AF.Exp, bias=negm[:]),
                    waits=[tm], inc=SC, tag="exp")
                last_exp[0] = te
                pt_ev = [None] * 8
                attnv_tok = [None] * 8
                for k in range(8):
                    bk = 4 + (k & 1)
                    off = (k & 1) * 1024
                    for j in range(4):
                        em("tensor", lambda e, k=k, j=j, off=off: e.transpose(
                            pC_bf[:, off + j * 128: off + (j + 1) * 128],
                            P_sb[:, (4 * k + j) * 128: (4 * k + j + 1) * 128], ident[:]),
                            waits=[te if (k == 0 and j == 0) else None,
                                   ((bank_free[bk] if k < 2 else pt_ev[k - 2])
                                    if j == 0 else None)],
                            inc=(T, 1) if j == 3 else None, tag="ptT")
                    t_pt = (T, T.n)
                    pe = em("vector", lambda e, k=k, off=off: e.tensor_copy(
                        PT[:, (k & 3) * 512:((k & 3) + 1) * 512], pC_bf[:, off:off + 512]),
                        waits=[t_pt, attnv_tok[k - 4] if k >= 4 else None],
                        inc=V, tag="ptev")
                    pt_ev[k] = pe
                    for cc in range(4):
                        c = 4 * k + cc
                        for nh in range(2):
                            em("tensor", lambda e, k=k, cc=cc, c=c, nh=nh: e.matmul(
                                pD[:, nh * 512:(nh + 1) * 512],
                                PT[:, (k & 3) * 512 + cc * 128: (k & 3) * 512 + (cc + 1) * 128],
                                XN[:, c * D + nh * 512: c * D + (nh + 1) * 512],
                                start=(c == 0), stop=(c == 31)),
                                waits=[pe if (cc == 0 and nh == 0) else None,
                                       t_xn_done if c == 0 else None,
                                       bank_free[6 + nh] if c == 0 else None],
                                tag="attnV")
                        em("tensor", lambda e, k=k, cc=cc, c=c: e.matmul(
                            pB[:, 512:513],
                            PT[:, (k & 3) * 512 + cc * 128: (k & 3) * 512 + (cc + 1) * 128],
                            onesc1[:], start=(c == 0), stop=(c == 31)),
                            waits=[(t_ones if c == 0 else None),
                                   (bank_free[3] if c == 0 else None)],
                            inc=(T, 1) if cc == 3 else None, tag="zr")
                    attnv_tok[k] = (T, T.n)
                t_u = attnv_tok[7]
                eu = em("vector", lambda e: e.tensor_copy(U_bf, pD[:]),
                        waits=[t_u], inc=V, tag="uev")
                bank_free[6] = eu
                bank_free[7] = eu
                em("vector", lambda e: e.tensor_copy(cs_bf[:, 0:1], pB[:, 512:513]),
                   waits=[t_u], tag="cs0")
                tcs = em("vector", lambda e: e.tensor_copy(cs_bf[:, 1:2], P_sb[:, 4096:4097]),
                         inc=V, tag="cs1")
                tct = em("tensor", lambda e: e.transpose(pB_bf[0:2, 0:128], cs_bf[:], ident[:]),
                         waits=[tcs, bank_free[2]], inc=(T, 1), tag="csT")
                tz = em("vector", lambda e: e.tensor_add(
                    Zf[:], pB[:, 512:513], P_sb[:, 4096:4097]), inc=V, tag="zadd")
                bank_free[3] = tz
                ecs = em("vector", lambda e: e.tensor_copy(csT[:], pB_bf[0:2, 0:128]),
                         waits=[tct], inc=V, tag="csTev")
                bank_free[2] = ecs
                em("vector", lambda e: e.reciprocal(rz[:], Zf[:]), tag="rz")
                ut_ev = [None, None]
                for kb in range(2):
                    off = kb * 1024
                    for j in range(4):
                        em("tensor", lambda e, kb=kb, j=j, off=off: e.transpose(
                            pC_bf[:, off + j * 128: off + (j + 1) * 128],
                            U_bf[:, (4 * kb + j) * 128: (4 * kb + j + 1) * 128], ident[:]),
                            waits=[eu if j == 0 else None,
                                   pt_ev[6 + kb] if j == 0 else None],
                            inc=(T, 1) if j == 3 else None, tag="utT")
                    t_ut = (T, T.n)
                    ue = em("vector", lambda e, kb=kb, off=off: e.tensor_copy(
                        UT[:, kb * 512:(kb + 1) * 512], pC_bf[:, off:off + 512]),
                        waits=[t_ut], inc=V, tag="utev")
                    ut_ev[kb] = ue
                bank_free[4] = ut_ev[0]
                bank_free[5] = ut_ev[1]
                for c in range(8):
                    for nh in range(2):
                        em("tensor", lambda e, c=c, nh=nh: e.matmul(
                            pA[:, nh * 512:(nh + 1) * 512], UT[:, c * 128:(c + 1) * 128],
                            wvT[:, c * D + nh * 512: c * D + (nh + 1) * 512],
                            start=(c == 0), stop=False),
                            waits=[ut_ev[c // 4] if nh == 0 else None,
                                   bank_free[nh] if c == 0 else None,
                                   ecs if c == 0 else None], tag="Y")
                for nh in range(2):
                    em("tensor", lambda e, nh=nh: e.matmul(
                        pA[:, nh * 512:(nh + 1) * 512], csT[:],
                        bvv0[:, nh * 512:(nh + 1) * 512], start=False, stop=True),
                        waits=[ecs, t_wrow], inc=(T, 1) if nh == 1 else None, tag="Yr2")
                t_y = (T, T.n)
                ey = em("vector", lambda e, t=t: e.scalar_tensor_tensor(
                    acc[:, t * D:(t + 1) * D], pA[:], rz[:], acc[:, t * D:(t + 1) * D],
                    op0=ALU.mult, op1=ALU.add),
                    waits=[t_y], inc=V, tag="accY")
                bank_free[0] = ey
                bank_free[1] = ey
            so = em("gpsimd", lambda e, g=g: e.dma_start(
                out=out_h[g * 512:(g + 1) * 512, :].rearrange("(t p) d -> p t d", p=128),
                in_=acc[:].rearrange("p (t d) -> p t d", t=4)),
                waits=[(V, V.n)], inc=dO, tag="store")
            acc_free = so
        em("gpsimd", lambda e: e.engine_nop(), waits=[(dO, dO.n)], tag="fin")

        P.check()
        with nc.Block() as block:
            P.realize(block)
    return nc


_NC_CACHE = [None]


def kernel(L0_attn_x, L0_attn_w_q, L0_attn_b_q, L0_attn_w_k, L0_attn_b_k,
           L0_attn_w_v, L0_attn_b_v, L0_attn_k_0, L0_attn_v_0,
           L0_ffn_U1, L0_ffn_V1, L0_ffn_b1, L0_ffn_U2, L0_ffn_V2, L0_ffn_b2):
    x = np.ascontiguousarray(np.asarray(L0_attn_x, dtype=np.float32))
    w = {
        "wq": np.ascontiguousarray(L0_attn_w_q, dtype=np.float32),
        "bq": np.ascontiguousarray(L0_attn_b_q, dtype=np.float32),
        "wk": np.ascontiguousarray(L0_attn_w_k, dtype=np.float32),
        "bk": np.ascontiguousarray(L0_attn_b_k, dtype=np.float32),
        "wv": np.ascontiguousarray(L0_attn_w_v, dtype=np.float32),
        "bv": np.ascontiguousarray(L0_attn_b_v, dtype=np.float32),
        "k0": np.ascontiguousarray(L0_attn_k_0, dtype=np.float32),
        "v0": np.ascontiguousarray(L0_attn_v_0, dtype=np.float32),
        "U1": np.ascontiguousarray(L0_ffn_U1, dtype=np.float32),
        "V1": np.ascontiguousarray(L0_ffn_V1, dtype=np.float32),
        "b1": np.ascontiguousarray(L0_ffn_b1, dtype=np.float32),
        "U2": np.ascontiguousarray(L0_ffn_U2, dtype=np.float32),
        "V2": np.ascontiguousarray(L0_ffn_V2, dtype=np.float32),
        "b2": np.ascontiguousarray(L0_ffn_b2, dtype=np.float32),
    }
    if _NC_CACHE[0] is None:
        _NC_CACHE[0] = build_nc()
    nc = _NC_CACHE[0]
    in_maps = []
    for core in range(8):
        b, h = core // 2, core % 2
        xb_ = x[b]
        xr = np.concatenate([xb_[h * QR:(h + 1) * QR], xb_[(1 - h) * QR:(2 - h) * QR]], axis=0)
        m = {"x": np.ascontiguousarray(xr)}
        m.update(w)
        in_maps.append(m)
    res = run_bass_kernel_spmd(nc, in_maps, core_ids=list(range(8)))
    out = np.empty((B, S, D), dtype=np.float32)
    for core in range(8):
        b, h = core // 2, core % 2
        out[b, h * QR:(h + 1) * QR] = res.results[core]["out"]
    return out


# version 08
# speedup vs baseline: 2.7160x; 2.7160x over previous
import numpy as np
import concourse.bass as bass
import concourse.mybir as mybir
from concourse.bass_utils import run_bass_kernel_spmd

F32 = mybir.dt.float32
BF16 = mybir.dt.bfloat16
AF = mybir.ActivationFunctionType
ALU = mybir.AluOpType
AX = mybir.AxisListType

B, S, D = 4, 4096, 1024
SK = S + 1
QR = 2048
NT1 = 32
NG1 = 8
NG2 = 4
EPS = 1e-5


class Ch:
    __slots__ = ("name", "n", "sem")
    def __init__(self, name):
        self.name, self.n, self.sem = name, 0, None


class Ev:
    __slots__ = ("eng", "fn", "waits", "inc", "tag")
    def __init__(self, eng, fn, waits, inc, tag):
        self.eng, self.fn, self.waits, self.inc, self.tag = eng, fn, waits, inc, tag


class Prog:
    def __init__(self):
        self.events = []

    def emit(self, eng, fn, waits=(), inc=None, tag=""):
        waits = [w for w in waits if w is not None]
        tok = None
        if inc is not None:
            if isinstance(inc, Ch):
                inc = (inc, 16 if inc.name.startswith("d") else 1)
            ch, k = inc
            ch.n += k
            tok = (ch, ch.n)
        self.events.append(Ev(eng, fn, waits, inc, tag))
        return tok

    def check(self):
        import collections
        q = collections.defaultdict(collections.deque)
        for ev in self.events:
            q[ev.eng].append(ev)
        vals = collections.defaultdict(int)
        progress = True
        while progress:
            progress = False
            for eng in list(q):
                while q[eng]:
                    ev = q[eng][0]
                    if all(vals[c.name] >= v for c, v in ev.waits):
                        q[eng].popleft()
                        if ev.inc is not None:
                            vals[ev.inc[0].name] += ev.inc[1]
                        progress = True
                    else:
                        break
        stuck = {e: (d[0].tag, [(c.name, v, vals[c.name]) for c, v in d[0].waits])
                 for e, d in q.items() if d}
        if stuck:
            raise RuntimeError(f"schedule deadlock: {stuck}")

    def realize(self, block):
        import collections
        marks = collections.defaultdict(int)

        def body(eng_name):
            def run(eng):
                for ev in self.events:
                    if ev.eng != eng_name:
                        continue
                    for ch, v in ev.waits:
                        if marks[(eng_name, ch.name)] < v:
                            eng.wait_ge(ch.sem, v)
                            marks[(eng_name, ch.name)] = v
                    inst = ev.fn(eng)
                    if ev.inc is not None:
                        inst.then_inc(ev.inc[0].sem, ev.inc[1])
            return run
        block.sync(body("sync"))
        block.gpsimd(body("gpsimd"))
        block.tensor(body("tensor"))
        block.vector(body("vector"))
        block.scalar(body("scalar"))


def build_nc():
    import contextlib
    import ml_dtypes
    nc = bass.Bass("TRN2", target_bir_lowering=False)

    x_h = nc.dram_tensor("x", [S, D], F32, kind="ExternalInput").ap()
    wq_h = nc.dram_tensor("wq", [129, D], F32, kind="ExternalInput").ap()
    bq_h = nc.dram_tensor("bq", [129], F32, kind="ExternalInput").ap()
    wk_h = nc.dram_tensor("wk", [129, D], F32, kind="ExternalInput").ap()
    bk_h = nc.dram_tensor("bk", [129], F32, kind="ExternalInput").ap()
    wv_h = nc.dram_tensor("wv", [D, D], F32, kind="ExternalInput").ap()
    bv_h = nc.dram_tensor("bv", [D], F32, kind="ExternalInput").ap()
    k0_h = nc.dram_tensor("k0", [129], F32, kind="ExternalInput").ap()
    v0_h = nc.dram_tensor("v0", [D], F32, kind="ExternalInput").ap()
    U1_h = nc.dram_tensor("U1", [D, 256], F32, kind="ExternalInput").ap()
    V1_h = nc.dram_tensor("V1", [256, 4096], F32, kind="ExternalInput").ap()
    b1_h = nc.dram_tensor("b1", [4096], F32, kind="ExternalInput").ap()
    U2_h = nc.dram_tensor("U2", [4096, 256], F32, kind="ExternalInput").ap()
    V2_h = nc.dram_tensor("V2", [256, D], F32, kind="ExternalInput").ap()
    b2_h = nc.dram_tensor("b2", [D], F32, kind="ExternalInput").ap()
    out_h = nc.dram_tensor("out", [QR, D], F32, kind="ExternalOutput").ap()
    ident_h = nc.inline_tensor(np.eye(128, dtype=ml_dtypes.bfloat16), "identity")

    ctx = contextlib.ExitStack()
    sb = lambda name, shape, dt: ctx.enter_context(nc.sbuf_tensor(name, shape, dt))
    psu = lambda name, shape, dt: ctx.enter_context(nc.psum_tensor(name, shape, dt))

    with ctx:
        XN = sb("XN", [128, NT1 * D], BF16)
        K_hi = sb("K_hi", [128, SK], BF16)
        K_lo = sb("K_lo", [1, SK], BF16)
        wqT = sb("wqT", [128, 8 * 129], BF16)
        wkT = sb("wkT", [128, 8 * 129], BF16)
        wvT = sb("wvT", [128, 8 * D], BF16)
        U1s = sb("U1s", [128, 8 * 256], BF16)
        V1s = sb("V1s", [128, 2 * 4096], BF16)
        U2s = sb("U2s", [128, 32 * 256], BF16)
        V2s = sb("V2s", [128, 2 * D], BF16)
        bvv0 = sb("bvv0", [2, D], BF16)
        b2row = sb("b2row", [1, D], BF16)
        b1T = sb("b1T", [128, 32], F32)
        bqT = sb("bqT", [128, 1], F32)
        bqL = sb("bqL", [1, 1], F32)
        bkT = sb("bkT", [128, 1], F32)
        bkL = sb("bkL", [1, 1], F32)
        ident = sb("ident", [128, 128], BF16)
        ones11 = sb("ones11", [1, 1], BF16)
        onescol = sb("onescol", [1, 128], BF16)
        onesc1 = sb("onesc1", [128, 1], BF16)
        scr = sb("scr", [128, 4100], F32)
        P_sb = sb("P_sb", [128, SK], BF16)
        PT = sb("PT", [128, 2048], BF16)
        xnT = sb("xnT", [128, 8 * 512], BF16)
        acc = sb("acc", [128, 4 * D], F32)
        xb = [sb("xb0", [128, D], F32), sb("xb1", [128, D], F32)]
        q_sb = sb("q_sb", [128, 128], BF16)
        qlo = sb("qlo", [1, 128], BF16)
        cs_bf = sb("cs_bf", [128, 2], BF16)
        csT = sb("csT", [2, 128], BF16)
        sum1 = sb("sum1", [128, 32], F32)
        sumsq = sb("sumsq", [128, 32], F32)
        mean = sb("mean", [128, 32], F32)
        msq = sb("msq", [128, 32], F32)
        varv = sb("varv", [128, 32], F32)
        lnv = sb("lnv", [128, 32], F32)
        rstd = sb("rstd", [128, 32], F32)
        nmrs = sb("nmrs", [128, 32], F32)
        maxp = sb("maxp", [128, 10], F32)
        negm = sb("negm", [128, 1], F32)
        Zf = sb("Zf", [128, 1], F32)
        rz = sb("rz", [128, 1], F32)
        spacer = sb("spacer", [128, 64], F32)
        sspace = sb("sspace", [128, 1], F32)
        k0f = sb("k0f", [128, 1], F32)
        k0l = sb("k0l", [1, 1], F32)
        eps_c = sb("eps_c", [128, 1], F32)
        nhalf_c = sb("nhalf_c", [128, 1], F32)

        xn_f32 = scr[:, 0:1024]
        Pf = scr[:, 1024:1536].bitcast(BF16)
        H2 = scr[:, 1536:2048].bitcast(BF16)
        Qf = scr[:, 2048:2560].bitcast(BF16)
        junkS = scr[:, 2560:3072].bitcast(BF16)
        junkV = scr[:, 3072:3584].bitcast(BF16)
        U_bf = P_sb[:, 0:1024]
        UT = P_sb[:, 1024:2048]
        lo_bf = P_sb[0:1, 2048:3072]
        hi_bf = P_sb[:, 0:1024]

        pA = psu("pA", [128, 1024], F32)
        pB = psu("pB", [128, 1024], F32)
        pC = psu("pC", [128, 1024], F32)
        pD = psu("pD", [128, 1024], F32)
        pB_bf = pB[:].bitcast(BF16)
        pC_bf = pC[:].bitcast(BF16)

        P = Prog()
        dX = [Ch("dX0"), Ch("dX1")]
        dW = [Ch("dW0"), Ch("dW1")]
        dWs, dO = Ch("dWs"), Ch("dO")
        V, SC, T = Ch("V"), Ch("S"), Ch("T")
        for c in (dX[0], dX[1], dW[0], dW[1], dWs, dO, V, SC, T):
            c.sem = ctx.enter_context(nc.semaphore(c.name))

        bank_free = {i: None for i in range(8)}
        em = P.emit

        def em2dma(eng, fn, waits=(), inc=None, tag=""):
            em(eng, fn, waits=waits, inc=inc, tag=tag)
            return em(eng, fn, inc=inc, tag=tag + "2")

        def vspace(tag="vsp"):
            em("vector", lambda e: e.memset(spacer[:], 0.0), tag=tag)

        em("vector", lambda e: e.memset(sspace[:], 1.0), tag="pre0")
        t_pre = em("vector", lambda e: e.memset(spacer[:], 1.0), inc=V, tag="pre1")
        em("scalar", lambda e: e.activation(sspace[:], sspace[:], AF.Copy),
           waits=[t_pre], inc=SC, tag="actgate")
        em("scalar", lambda e: e.activation(sspace[:], sspace[:], AF.Exp),
           inc=SC, tag="actgate2")
        t_actgate = (SC, SC.n)
        em("vector", lambda e: e.reciprocal(spacer[:, 0:1], spacer[:, 0:1]),
           waits=[t_actgate], inc=V, tag="dvegate")
        t_dvegate = (V, V.n)
        for eng in ("sync", "gpsimd", "tensor"):
            em(eng, lambda e: e.nop(), waits=[t_actgate, t_dvegate], tag="bootw")
        em("gpsimd", lambda e: e.dma_start(out=xb[0][:, 0:512], in_=x_h[0:128, 0:512]),
           inc=dW[0], tag="boot")
        em("gpsimd", lambda e: e.dma_start(out=xb[1][:, 0:512], in_=x_h[0:128, 0:512]),
           inc=dW[1], tag="boot")
        em("gpsimd", lambda e: e.dma_start(out=spacer[:], in_=x_h[0:128, 0:64]),
           inc=dWs, tag="boot")
        em("sync", lambda e: e.dma_start(out=xb[0][:, 512:1024], in_=x_h[0:128, 512:1024]),
           inc=dX[0], tag="boot")
        em("sync", lambda e: e.dma_start(out=xb[1][:, 512:1024], in_=x_h[0:128, 512:1024]),
           inc=dX[1], tag="boot")
        t_bw = [(dW[0], dW[0].n), (dW[1], dW[1].n), (dWs, dWs.n),
                (dX[0], dX[0].n), (dX[1], dX[1].n)]
        em("vector", lambda e: e.reduce_sum(sum1[:, 0:1], xb[0][:], axis=AX.X),
           waits=t_bw, inc=V, tag="bootrd")
        em("vector", lambda e: e.reduce_sum(sum1[:, 1:2], xb[1][:], axis=AX.X),
           inc=V, tag="bootrd")
        t_boot = (V, V.n)

        em2dma("gpsimd", lambda e: e.dma_start(out=ident[:], in_=ident_h.ap()),
           waits=[t_boot], inc=dWs)
        em2dma("gpsimd", lambda e: e.dma_start(
            out=bqT[:], in_=bq_h[0:128].rearrange("(p o) -> p o", o=1),
            allow_slow_non_contiguous=True), inc=dWs)
        em2dma("gpsimd", lambda e: e.dma_start(
            out=bqL[:], in_=bq_h[128:129].rearrange("(p o) -> p o", o=1)), inc=dWs)
        em2dma("gpsimd", lambda e: e.dma_start(
            out=bkT[:], in_=bk_h[0:128].rearrange("(p o) -> p o", o=1),
            allow_slow_non_contiguous=True), inc=dWs)
        em2dma("gpsimd", lambda e: e.dma_start(
            out=bkL[:], in_=bk_h[128:129].rearrange("(p o) -> p o", o=1)), inc=dWs)
        em2dma("gpsimd", lambda e: e.dma_start(
            out=b1T[:], in_=b1_h.rearrange("(c p) -> p c", p=128),
            allow_slow_non_contiguous=True), inc=dWs)
        em2dma("gpsimd", lambda e: e.dma_start(
            out=k0f[:], in_=k0_h[0:128].rearrange("(p o) -> p o", o=1),
            allow_slow_non_contiguous=True), inc=dWs)
        em2dma("gpsimd", lambda e: e.dma_start(
            out=k0l[:], in_=k0_h[128:129].rearrange("(p o) -> p o", o=1)), inc=dWs)
        em2dma("gpsimd", lambda e: e.dma_start(
            out=b2row[0:1, :], in_=b2_h.rearrange("(o d) -> o d", o=1)), inc=dWs)
        em2dma("gpsimd", lambda e: e.dma_start(
            out=bvv0[0:1, :], in_=bv_h.rearrange("(o d) -> o d", o=1)), inc=dWs)
        em2dma("gpsimd", lambda e: e.dma_start(
            out=bvv0[1:2, :], in_=v0_h.rearrange("(o d) -> o d", o=1)), inc=dWs)
        t_smalls = (dWs, dWs.n)
        t_ident = t_smalls
        t_bias = t_smalls
        t_wrow = t_smalls
        em("vector", lambda e: e.memset(ones11[:], 1.0), inc=V)
        em("vector", lambda e: e.memset(onescol[:], 1.0), inc=V)
        em("vector", lambda e: e.memset(onesc1[:], 1.0), inc=V)
        t_ones = (V, V.n)
        em("vector", lambda e: e.memset(eps_c[:], EPS), inc=V)
        em("vector", lambda e: e.memset(nhalf_c[:], -0.5), inc=V)
        t_consts = (V, V.n)
        em("vector", lambda e: e.tensor_copy(K_hi[:, 4096:4097], k0f[:]),
           waits=[t_smalls], inc=V)
        em("vector", lambda e: e.tensor_copy(K_lo[0:1, 4096:4097], k0l[:]), inc=V)
        t_k0 = (V, V.n)

        stage_tok = [t_boot, t_boot]
        load_tok = [None, None]
        stage_i = [0]

        def stage_load(src_ap, rows=128, cols=D):
            i = stage_i[0] & 1
            stage_i[0] += 1
            dst = xb[i][0:rows, 0:cols]
            tk = em2dma("gpsimd", lambda e, d=dst, s=src_ap: e.dma_start(out=d, in_=s),
                    waits=[stage_tok[i]], inc=dW[i], tag="wload")
            load_tok[i] = tk
            return i, dst

        def load_cast(src_ap, dst_bf, rows=128, cols=D, extra=None):
            i, st = stage_load(src_ap, rows, cols)
            tk = em("vector", lambda e, d=dst_bf, s=st: e.tensor_copy(d, s),
                    waits=[load_tok[i], extra], inc=V, tag="wcast")
            stage_tok[i] = tk
            return tk

        for c in range(8):
            load_cast(U1_h[c * 128:(c + 1) * 128, :], U1s[:, c * 256:(c + 1) * 256], cols=256)
        for m in range(2):
            for pce in range(4):
                load_cast(V1_h[m * 128:(m + 1) * 128, pce * 1024:(pce + 1) * 1024],
                          V1s[:, m * 4096 + pce * 1024: m * 4096 + (pce + 1) * 1024])
        for c in range(32):
            load_cast(U2_h[c * 128:(c + 1) * 128, :], U2s[:, c * 256:(c + 1) * 256], cols=256)
        for m in range(2):
            load_cast(V2_h[m * 128:(m + 1) * 128, :], V2s[:, m * D:(m + 1) * D])

        def prep_qk(w_h, wT):
            tcast = load_cast(w_h[0:128, :], hi_bf, extra=bank_free[4])
            for c in range(8):
                em("tensor", lambda e, c=c: e.transpose(
                    pC_bf[:, c * 128:(c + 1) * 128], hi_bf[:, c * 128:(c + 1) * 128], ident[:]),
                    waits=[tcast, t_ident, bank_free[4] if c == 0 else None],
                    inc=(T, 1) if c == 7 else None, tag="qkT")
            t_tr = (T, T.n)
            wT3 = wT[:].rearrange("p (c k) -> p c k", k=129)
            pc3 = pC_bf[:, 0:1024].rearrange("p (c k) -> p c k", k=128)
            ev = em("vector", lambda e, w3=wT3, p3=pc3: e.tensor_copy(w3[:, :, 0:128], p3),
                    waits=[t_tr], inc=V, tag="qkTev")
            bank_free[4] = ev
            tc2 = load_cast(w_h[128:129, :], lo_bf[0:1, :], rows=1, extra=bank_free[6])
            for c in range(8):
                em("tensor", lambda e, c=c: e.matmul(
                    pD[:, c:c + 1], lo_bf[0:1, c * 128:(c + 1) * 128], ones11[:],
                    start=True, stop=True),
                    waits=[tc2, t_ones, bank_free[6] if c == 0 else None],
                    inc=(T, 1) if c == 7 else None, tag="qklo")
            t_lo = (T, T.n)
            ev2 = em("vector", lambda e, w3=wT3: e.tensor_copy(
                w3[:, :, 128:129], pD[:, 0:8].rearrange("p (c k) -> p c k", k=1)),
                waits=[t_lo], inc=V, tag="qkloev")
            bank_free[6] = ev2

        prep_qk(wq_h, wqT)
        prep_qk(wk_h, wkT)

        wv3 = wvT[:].rearrange("p (dc e) -> p dc e", e=1024)
        for ec in range(8):
            tcast = load_cast(wv_h[ec * 128:(ec + 1) * 128, :], hi_bf, extra=bank_free[4])
            for c in range(8):
                em("tensor", lambda e, c=c: e.transpose(
                    pC_bf[:, c * 128:(c + 1) * 128], hi_bf[:, c * 128:(c + 1) * 128], ident[:]),
                    waits=[tcast, bank_free[4] if c == 0 else None],
                    inc=(T, 1) if c == 7 else None, tag="wvT")
            t_tr = (T, T.n)
            ev = em("vector", lambda e, ec=ec: e.tensor_copy(
                wv3[:, :, ec * 128:(ec + 1) * 128],
                pC_bf[:, 0:1024].rearrange("p (c k) -> p c k", k=128)),
                waits=[t_tr], inc=V, tag="wvTev")
            bank_free[4] = ev

        def ln_stats(x_t, waits, t):
            em("vector", lambda e, t=t: e.scalar_tensor_tensor(
                junkV, x_t, 1.0, x_t, op0=ALU.mult, op1=ALU.mult,
                accum_out=sumsq[:, t:t + 1]),
                waits=waits, inc=V, tag="lnsq")
            tk = em("vector", lambda e, t=t: e.reduce_sum(
                sum1[:, t:t + 1], x_t, axis=AX.X), inc=V, tag="lnsum")
            return tk

        def vgap(n=6):
            for _ in range(n):
                em("vector", lambda e: e.memset(junkV, 0.0), tag="vgap")

        def sgap(n=2):
            for _ in range(n):
                em("scalar", lambda e: e.activation(sspace[:], sspace[:], AF.Copy),
                   tag="sgap")

        def ln_chain_batched(t_stats_done):
            em("vector", lambda e: e.tensor_scalar_mul(mean[:], sum1[:], 1.0 / D),
               waits=[t_stats_done], tag="lnmean")
            em("vector", lambda e: e.tensor_scalar_mul(msq[:], sumsq[:], 1.0 / D),
               tag="lnmsq")
            vgap()
            em("vector", lambda e: e.tensor_mul(varv[:], mean[:], mean[:]), tag="lnm2")
            vgap()
            tv2 = em("vector", lambda e: e.tensor_sub(varv[:], msq[:], varv[:]),
                     inc=V, tag="lnvar")
            sgap()
            em("scalar", lambda e: e.activation(lnv[:], varv[:], AF.Ln, bias=eps_c[:]),
               waits=[tv2, t_consts], tag="lnln")
            sgap()
            em("scalar", lambda e: e.activation(rstd[:], lnv[:], AF.Exp, scale=nhalf_c[:]),
               tag="lnexp")
            tr = em("scalar", lambda e: e.activation(sspace[:], sspace[:], AF.Copy),
                    inc=SC, tag="lnexp_m")
            vgap()
            tn = em("vector", lambda e: e.scalar_tensor_tensor(
                nmrs[:], mean[:], -1.0, rstd[:], op0=ALU.mult, op1=ALU.mult),
                waits=[tr], inc=V, tag="lnnm")
            tnm = em("vector", lambda e: e.memset(junkV, 0.0), inc=V, tag="lnnm_m")
            return tnm

        def ln_affine(x_t, waits, t, out_ap):
            return em("scalar", lambda e, o=out_ap, t=t: e.activation(
                o, x_t, AF.Identity, bias=nmrs[:, t:t + 1], scale=rstd[:, t:t + 1]),
                waits=waits, inc=SC, tag="lnaff")

        def xnt_group(tiles, src_toks):
            for c in range(8):
                bk = 4 + (c & 1)
                off = (c & 1) * 1024
                for j, tl in enumerate(tiles):
                    em("tensor", lambda e, c=c, tl=tl, off=off, j=j: e.transpose(
                        pC_bf[:, off + j * 128: off + (j + 1) * 128],
                        XN[:, tl * D + c * 128: tl * D + (c + 1) * 128], ident[:]),
                        waits=[src_toks[j], bank_free[bk] if j == 0 else None],
                        inc=(T, 1) if j == 3 else None, tag="xnT")
                t_tr = (T, T.n)
                ev = em("vector", lambda e, c=c, off=off: e.tensor_copy(
                    xnT[:, c * 512:(c + 1) * 512], pC_bf[:, off:off + 512]),
                    waits=[t_tr], inc=V, tag="xnTev")
                bank_free[bk] = ev
            return (V, V.n)

        xb_tok = [stage_tok[0], stage_tok[1]]
        xn_tok = [None] * NT1
        xload_tok = [None] * (NT1 + 1)
        for t in range(NT1):
            i = t & 1
            xload_tok[t] = em2dma("sync", lambda e, t=t, i=i: e.dma_start(
                out=xb[i][:], in_=x_h[t * 128:(t + 1) * 128, :]),
                waits=[xb_tok[i]], inc=dX[i], tag="xloadA")
            if t > 0:
                xb_tok[(t - 1) & 1] = ln_stats(
                    xb[(t - 1) & 1][:], [xload_tok[t - 1], xload_tok[t]], t - 1)
        xload_tok[NT1] = em2dma("sync", lambda e: e.dma_start(
            out=xb[0][:], in_=x_h[0:128, :]), waits=[xb_tok[0]], inc=dX[0],
            tag="xloadAd")
        xb_tok[0] = xload_tok[NT1]
        xb_tok[1] = ln_stats(xb[1][:], [xload_tok[NT1 - 1], xload_tok[NT1]], NT1 - 1)
        t_chain = ln_chain_batched(xb_tok[1])
        def kproj_group(g):
            t_xnt = xnt_group([g * 4 + t for t in range(4)],
                              [xn_tok[g * 4 + t] for t in range(4)])
            for c in range(8):
                em("tensor", lambda e, c=c: e.matmul(
                    pA[:, 0:512], wkT[:, c * 129:c * 129 + 128],
                    xnT[:, c * 512:(c + 1) * 512],
                    start=(c == 0), stop=(c == 7)),
                    waits=[t_xnt, bank_free[0] if c == 0 else None], tag="kprojhi")
                em("tensor", lambda e, c=c: e.matmul(
                    pA[0:1, 512:1024], wkT[:, c * 129 + 128:c * 129 + 129],
                    xnT[:, c * 512:(c + 1) * 512],
                    start=(c == 0), stop=(c == 7)),
                    waits=[bank_free[1] if c == 0 else None],
                    inc=(T, 1) if c == 7 else None, tag="kprojlo")
            t_k = (T, T.n)
            e1 = em("scalar", lambda e, g=g: e.activation(
                K_hi[:, g * 512:(g + 1) * 512], pA[:, 0:512], AF.Identity, bias=bkT[:]),
                waits=[t_k, t_bias], inc=SC, tag="kevhi")
            e2 = em("scalar", lambda e, g=g: e.activation(
                K_lo[0:1, g * 512:(g + 1) * 512], pA[0:1, 512:1024], AF.Identity,
                bias=bkL[:]),
                inc=SC, tag="kevlo")
            bank_free[0] = e1
            bank_free[1] = e2

        for idx in range(NT1 + 1):
            i = idx & 1
            tsrc = 0 if idx == NT1 else idx
            xload_tok[idx] = em2dma("sync", lambda e, tsrc=tsrc, i=i: e.dma_start(
                out=xb[i][:], in_=x_h[tsrc * 128:(tsrc + 1) * 128, :]),
                waits=[xb_tok[i]], inc=dX[i], tag="xloadB")
            if idx == NT1:
                xb_tok[0] = xload_tok[idx]
            if idx > 0:
                tile = idx - 1
                pt = tile & 1
                ta = ln_affine(xb[pt][:], [xload_tok[tile], xload_tok[idx], t_chain],
                               tile, XN[:, tile * D:(tile + 1) * D])
                xb_tok[pt] = ta
                xn_tok[tile] = ta
            if idx > 0 and idx % 4 == 0:
                kproj_group(idx // 4 - 1)
        t_xn_done = xn_tok[NT1 - 1]
        t_k_done = (SC, SC.n)

        acc_free = None
        last_exp = [None]
        prev_tadd = [None]
        for g in range(NG2):
            acc_toks = [None] * 4
            for t in range(4):
                tile = g * 4 + t
                i = tile & 1
                xload_tok[tile] = em2dma("sync", lambda e, tile=tile, i=i: e.dma_start(
                    out=xb[i][:], in_=x_h[tile * 128:(tile + 1) * 128, :]),
                    waits=[xb_tok[i]], inc=dX[i], tag="xloadC")
                if t > 0:
                    tile_p = tile - 1
                    pt = tile_p & 1
                    ta = ln_affine(xb[pt][:],
                                   [xload_tok[tile_p], xload_tok[tile], t_chain,
                                    prev_tadd[0]],
                                   tile_p, xn_f32)
                    xn_tok[tile_p] = ta
                    tadd = em("vector", lambda e, t=t, pt=pt: e.tensor_add(
                        acc[:, (t - 1) * D:t * D], xb[pt][:], xn_f32),
                        waits=[ta, acc_free], inc=V, tag="accinit")
                    xb_tok[pt] = tadd
                    acc_toks[t - 1] = tadd
                    prev_tadd[0] = tadd
            tile_p = g * 4 + 3
            pt = tile_p & 1
            idum = 1 - pt
            tdum = em2dma("sync", lambda e, idum=idum: e.dma_start(
                out=xb[idum][:], in_=x_h[0:128, :]),
                waits=[xb_tok[idum]], inc=dX[idum], tag="xloadCd")
            xb_tok[idum] = tdum
            ta = ln_affine(xb[pt][:], [xload_tok[tile_p], tdum, t_chain,
                                       prev_tadd[0]],
                           tile_p, xn_f32)
            xn_tok[tile_p] = ta
            tadd = em("vector", lambda e, pt=pt: e.tensor_add(
                acc[:, 3 * D:4 * D], xb[pt][:], xn_f32),
                waits=[ta, acc_free], inc=V, tag="accinit")
            xb_tok[pt] = tadd
            acc_toks[3] = tadd
            prev_tadd[0] = tadd
            t_xnt = xnt_group([g * 4 + t for t in range(4)],
                              [xn_tok[g * 4 + t] for t in range(4)])
            for c in range(8):
                for m in range(2):
                    em("tensor", lambda e, c=c, m=m: e.matmul(
                        pA[:, m * 512:(m + 1) * 512],
                        U1s[:, c * 256 + m * 128: c * 256 + (m + 1) * 128],
                        xnT[:, c * 512:(c + 1) * 512], start=(c == 0), stop=(c == 7)),
                        waits=[t_xnt if c == 0 else None,
                               bank_free[m] if c == 0 else None],
                        inc=(T, 1) if (c == 7 and m == 1) else None, tag="ffnP")
            t_p = (T, T.n)
            ep = em("scalar", lambda e: e.activation(Pf, pA[:], AF.Copy),
                    waits=[t_p], inc=SC, tag="ffnPev")
            bank_free[0] = ep
            bank_free[1] = ep
            gelu_tok = [None, None]
            for j in range(32):
                jb = j & 1
                em("tensor", lambda e, j=j, jb=jb: e.matmul(
                    pB[:, jb * 512:(jb + 1) * 512],
                    V1s[:, 0 * 4096 + j * 128:0 * 4096 + (j + 1) * 128],
                    Pf[:, 0:512], start=True, stop=False),
                    waits=[ep if j < 2 else None,
                           (gelu_tok[jb] if j >= 2 else bank_free[2 + jb])], tag="ffnH")
                em("tensor", lambda e, j=j, jb=jb: e.matmul(
                    pB[:, jb * 512:(jb + 1) * 512],
                    V1s[:, 1 * 4096 + j * 128:1 * 4096 + (j + 1) * 128],
                    Pf[:, 512:1024], start=False, stop=True),
                    inc=(T, 1), tag="ffnH2")
                t_h = (T, T.n)
                gt = em("scalar", lambda e, j=j, jb=jb: e.activation(
                    H2[:, jb * 512:(jb + 1) * 512], pB[:, jb * 512:(jb + 1) * 512],
                    AF.Gelu_apprx_tanh, bias=b1T[:, j:j + 1]),
                    waits=[t_h, t_smalls], inc=SC, tag="gelu")
                gelu_tok[jb] = gt
                for m in range(2):
                    em("tensor", lambda e, j=j, jb=jb, m=m: e.matmul(
                        pC[:, m * 512:(m + 1) * 512],
                        U2s[:, j * 256 + m * 128:j * 256 + (m + 1) * 128],
                        H2[:, jb * 512:(jb + 1) * 512], start=(j == 0), stop=(j == 31)),
                        waits=[gt, (bank_free[4 + m] if j == 0 else None)],
                        inc=(T, 1) if (j == 31 and m == 1) else None, tag="ffnQ")
            bank_free[2] = gelu_tok[0]
            bank_free[3] = gelu_tok[1]
            t_q = (T, T.n)
            eq = em("vector", lambda e: e.tensor_copy(Qf, pC[:]),
                    waits=[t_q], inc=V, tag="ffnQev")
            bank_free[4] = eq
            bank_free[5] = eq
            for t in range(4):
                for nh in range(2):
                    for m in range(2):
                        em("tensor", lambda e, t=t, nh=nh, m=m: e.matmul(
                            pD[:, nh * 512:(nh + 1) * 512],
                            Qf[:, m * 512 + t * 128: m * 512 + (t + 1) * 128],
                            V2s[:, m * D + nh * 512: m * D + (nh + 1) * 512],
                            start=(m == 0), stop=False),
                            waits=[eq if m == 0 else None,
                                   bank_free[6 + nh] if m == 0 else None], tag="ffnO")
                    em("tensor", lambda e, nh=nh: e.matmul(
                        pD[:, nh * 512:(nh + 1) * 512], onescol[:],
                        b2row[0:1, nh * 512:(nh + 1) * 512], start=False, stop=True),
                        waits=[t_ones, t_wrow],
                        inc=(T, 1) if nh == 1 else None, tag="ffnOb")
                t_o = (T, T.n)
                eo = em("vector", lambda e, t=t: e.tensor_add(
                    acc[:, t * D:(t + 1) * D], acc[:, t * D:(t + 1) * D], pD[:]),
                    waits=[t_o, acc_toks[t]], inc=V, tag="accO")
                bank_free[6] = eo
                bank_free[7] = eo
            for t in range(4):
                for c in range(8):
                    em("tensor", lambda e, c=c, t=t: e.matmul(
                        pA[:, 0:128], wqT[:, c * 129:c * 129 + 128],
                        xnT[:, c * 512 + t * 128: c * 512 + (t + 1) * 128],
                        start=(c == 0), stop=(c == 7)),
                        waits=[bank_free[0] if c == 0 else None], tag="qhi")
                    em("tensor", lambda e, c=c, t=t: e.matmul(
                        pA[0:1, 512:640], wqT[:, c * 129 + 128:c * 129 + 129],
                        xnT[:, c * 512 + t * 128: c * 512 + (t + 1) * 128],
                        start=(c == 0), stop=(c == 7)),
                        waits=[bank_free[1] if c == 0 else None],
                        inc=(T, 1) if c == 7 else None, tag="qlo")
                t_qp = (T, T.n)
                eqh = em("scalar", lambda e: e.activation(
                    q_sb[:], pA[:, 0:128], AF.Identity, bias=bqT[:]),
                    waits=[t_qp, t_bias], inc=SC, tag="qevhi")
                eql = em("scalar", lambda e: e.activation(
                    qlo[:], pA[0:1, 512:640], AF.Identity, bias=bqL[:]),
                    inc=SC, tag="qevlo")
                bank_free[0] = eqh
                slab_ev = [None] * 8
                for s_ in range(8):
                    sb_ = s_ & 1
                    em("tensor", lambda e, s_=s_, sb_=sb_: e.matmul(
                        pB[:, sb_ * 512:(sb_ + 1) * 512], q_sb[:],
                        K_hi[:, s_ * 512:(s_ + 1) * 512],
                        start=True, stop=False),
                        waits=[eqh,
                               t_k_done if s_ == 0 else None,
                               (slab_ev[s_ - 2] if s_ >= 2 else bank_free[2 + sb_])],
                        tag="schi")
                    em("tensor", lambda e, s_=s_, sb_=sb_: e.matmul(
                        pB[:, sb_ * 512:(sb_ + 1) * 512], qlo[:],
                        K_lo[0:1, s_ * 512:(s_ + 1) * 512],
                        start=False, stop=True),
                        waits=[eql], inc=(T, 1), tag="sclo")
                    t_sc = (T, T.n)
                    se = em("scalar", lambda e, s_=s_, sb_=sb_: e.activation(
                        scr[:, s_ * 512:(s_ + 1) * 512], pB[:, sb_ * 512:(sb_ + 1) * 512],
                        AF.Copy),
                        waits=[t_sc, acc_toks[3] if (t == 0 and s_ == 0) else None],
                        inc=SC, tag="scev")
                    slab_ev[s_] = se
                    em("vector", lambda e, s_=s_: e.reduce_max(
                        maxp[:, s_:s_ + 1], scr[:, s_ * 512:(s_ + 1) * 512], axis=AX.X),
                        waits=[se], tag="maxp")
                bank_free[2] = slab_ev[6]
                bank_free[3] = slab_ev[7]
                em("tensor", lambda e: e.matmul(
                    pA[:, 640:641], q_sb[:], K_hi[:, 4096:4097], start=True, stop=False),
                    waits=[eql, t_k0], tag="snkhi")
                em("tensor", lambda e: e.matmul(
                    pA[:, 640:641], qlo[:], K_lo[0:1, 4096:4097], start=False, stop=True),
                    inc=(T, 1), tag="snklo")
                t_snk = (T, T.n)
                es = em("scalar", lambda e: e.activation(
                    scr[:, 4096:4097], pA[:, 640:641], AF.Copy),
                    waits=[t_snk], inc=SC, tag="snkev")
                bank_free[1] = es
                em("vector", lambda e: e.tensor_copy(maxp[:, 8:9], scr[:, 4096:4097]),
                   waits=[es], tag="snkmax")
                vspace("vsp_negm")
                tm = em("vector", lambda e: e.reduce_max(
                    negm[:], maxp[:, 0:9], axis=AX.X, negate=True), inc=V, tag="negm")
                te = em("scalar", lambda e: e.activation(
                    P_sb[:, 0:SK], scr[:, 0:SK], AF.Exp, bias=negm[:]),
                    waits=[tm], inc=SC, tag="exp")
                last_exp[0] = te
                pt_ev = [None] * 8
                attnv_tok = [None] * 8
                for k in range(8):
                    bk = 4 + (k & 1)
                    off = (k & 1) * 1024
                    for j in range(4):
                        em("tensor", lambda e, k=k, j=j, off=off: e.transpose(
                            pC_bf[:, off + j * 128: off + (j + 1) * 128],
                            P_sb[:, (4 * k + j) * 128: (4 * k + j + 1) * 128], ident[:]),
                            waits=[te if (k == 0 and j == 0) else None,
                                   ((bank_free[bk] if k < 2 else pt_ev[k - 2])
                                    if j == 0 else None)],
                            inc=(T, 1) if j == 3 else None, tag="ptT")
                    t_pt = (T, T.n)
                    pe = em("vector", lambda e, k=k, off=off: e.tensor_copy(
                        PT[:, (k & 3) * 512:((k & 3) + 1) * 512], pC_bf[:, off:off + 512]),
                        waits=[t_pt, attnv_tok[k - 4] if k >= 4 else None],
                        inc=V, tag="ptev")
                    pt_ev[k] = pe
                    for cc in range(4):
                        c = 4 * k + cc
                        for nh in range(2):
                            em("tensor", lambda e, k=k, cc=cc, c=c, nh=nh: e.matmul(
                                pD[:, nh * 512:(nh + 1) * 512],
                                PT[:, (k & 3) * 512 + cc * 128: (k & 3) * 512 + (cc + 1) * 128],
                                XN[:, c * D + nh * 512: c * D + (nh + 1) * 512],
                                start=(c == 0), stop=(c == 31)),
                                waits=[pe if (cc == 0 and nh == 0) else None,
                                       t_xn_done if c == 0 else None,
                                       bank_free[6 + nh] if c == 0 else None],
                                tag="attnV")
                        em("tensor", lambda e, k=k, cc=cc, c=c: e.matmul(
                            pB[:, 512:513],
                            PT[:, (k & 3) * 512 + cc * 128: (k & 3) * 512 + (cc + 1) * 128],
                            onesc1[:], start=(c == 0), stop=(c == 31)),
                            waits=[(t_ones if c == 0 else None),
                                   (bank_free[3] if c == 0 else None)],
                            inc=(T, 1) if cc == 3 else None, tag="zr")
                    attnv_tok[k] = (T, T.n)
                t_u = attnv_tok[7]
                eu = em("vector", lambda e: e.tensor_copy(U_bf, pD[:]),
                        waits=[t_u], inc=V, tag="uev")
                bank_free[6] = eu
                bank_free[7] = eu
                em("vector", lambda e: e.tensor_copy(cs_bf[:, 0:1], pB[:, 512:513]),
                   waits=[t_u], tag="cs0")
                tcs = em("vector", lambda e: e.tensor_copy(cs_bf[:, 1:2], P_sb[:, 4096:4097]),
                         inc=V, tag="cs1")
                tct = em("tensor", lambda e: e.transpose(pB_bf[0:2, 0:128], cs_bf[:], ident[:]),
                         waits=[tcs, bank_free[2]], inc=(T, 1), tag="csT")
                tz = em("vector", lambda e: e.tensor_add(
                    Zf[:], pB[:, 512:513], P_sb[:, 4096:4097]), inc=V, tag="zadd")
                bank_free[3] = tz
                ecs = em("vector", lambda e: e.tensor_copy(csT[:], pB_bf[0:2, 0:128]),
                         waits=[tct], inc=V, tag="csTev")
                bank_free[2] = ecs
                em("vector", lambda e: e.reciprocal(rz[:], Zf[:]), tag="rz")
                ut_ev = [None, None]
                for kb in range(2):
                    off = kb * 1024
                    for j in range(4):
                        em("tensor", lambda e, kb=kb, j=j, off=off: e.transpose(
                            pC_bf[:, off + j * 128: off + (j + 1) * 128],
                            U_bf[:, (4 * kb + j) * 128: (4 * kb + j + 1) * 128], ident[:]),
                            waits=[eu if j == 0 else None,
                                   pt_ev[6 + kb] if j == 0 else None],
                            inc=(T, 1) if j == 3 else None, tag="utT")
                    t_ut = (T, T.n)
                    ue = em("vector", lambda e, kb=kb, off=off: e.tensor_copy(
                        UT[:, kb * 512:(kb + 1) * 512], pC_bf[:, off:off + 512]),
                        waits=[t_ut], inc=V, tag="utev")
                    ut_ev[kb] = ue
                bank_free[4] = ut_ev[0]
                bank_free[5] = ut_ev[1]
                for c in range(8):
                    for nh in range(2):
                        em("tensor", lambda e, c=c, nh=nh: e.matmul(
                            pA[:, nh * 512:(nh + 1) * 512], UT[:, c * 128:(c + 1) * 128],
                            wvT[:, c * D + nh * 512: c * D + (nh + 1) * 512],
                            start=(c == 0), stop=False),
                            waits=[ut_ev[c // 4] if nh == 0 else None,
                                   bank_free[nh] if c == 0 else None,
                                   ecs if c == 0 else None], tag="Y")
                for nh in range(2):
                    em("tensor", lambda e, nh=nh: e.matmul(
                        pA[:, nh * 512:(nh + 1) * 512], csT[:],
                        bvv0[:, nh * 512:(nh + 1) * 512], start=False, stop=True),
                        waits=[ecs, t_wrow], inc=(T, 1) if nh == 1 else None, tag="Yr2")
                t_y = (T, T.n)
                ey = em("vector", lambda e, t=t: e.scalar_tensor_tensor(
                    acc[:, t * D:(t + 1) * D], pA[:], rz[:], acc[:, t * D:(t + 1) * D],
                    op0=ALU.mult, op1=ALU.add),
                    waits=[t_y], inc=V, tag="accY")
                bank_free[0] = ey
                bank_free[1] = ey
            so = em("gpsimd", lambda e, g=g: e.dma_start(
                out=out_h[g * 512:(g + 1) * 512, :].rearrange("(t p) d -> p t d", p=128),
                in_=acc[:].rearrange("p (t d) -> p t d", t=4)),
                waits=[(V, V.n)], inc=dO, tag="store")
            acc_free = so
        em("gpsimd", lambda e: e.engine_nop(), waits=[(dO, dO.n)], tag="fin")

        P.check()
        with nc.Block() as block:
            P.realize(block)
    return nc


_NC_CACHE = [None]


def kernel(L0_attn_x, L0_attn_w_q, L0_attn_b_q, L0_attn_w_k, L0_attn_b_k,
           L0_attn_w_v, L0_attn_b_v, L0_attn_k_0, L0_attn_v_0,
           L0_ffn_U1, L0_ffn_V1, L0_ffn_b1, L0_ffn_U2, L0_ffn_V2, L0_ffn_b2):
    x = np.ascontiguousarray(np.asarray(L0_attn_x, dtype=np.float32))
    w = {
        "wq": np.ascontiguousarray(L0_attn_w_q, dtype=np.float32),
        "bq": np.ascontiguousarray(L0_attn_b_q, dtype=np.float32),
        "wk": np.ascontiguousarray(L0_attn_w_k, dtype=np.float32),
        "bk": np.ascontiguousarray(L0_attn_b_k, dtype=np.float32),
        "wv": np.ascontiguousarray(L0_attn_w_v, dtype=np.float32),
        "bv": np.ascontiguousarray(L0_attn_b_v, dtype=np.float32),
        "k0": np.ascontiguousarray(L0_attn_k_0, dtype=np.float32),
        "v0": np.ascontiguousarray(L0_attn_v_0, dtype=np.float32),
        "U1": np.ascontiguousarray(L0_ffn_U1, dtype=np.float32),
        "V1": np.ascontiguousarray(L0_ffn_V1, dtype=np.float32),
        "b1": np.ascontiguousarray(L0_ffn_b1, dtype=np.float32),
        "U2": np.ascontiguousarray(L0_ffn_U2, dtype=np.float32),
        "V2": np.ascontiguousarray(L0_ffn_V2, dtype=np.float32),
        "b2": np.ascontiguousarray(L0_ffn_b2, dtype=np.float32),
    }
    if _NC_CACHE[0] is None:
        _NC_CACHE[0] = build_nc()
    nc = _NC_CACHE[0]
    in_maps = []
    for core in range(8):
        b, h = core // 2, core % 2
        xb_ = x[b]
        xr = np.concatenate([xb_[h * QR:(h + 1) * QR], xb_[(1 - h) * QR:(2 - h) * QR]], axis=0)
        m = {"x": np.ascontiguousarray(xr)}
        m.update(w)
        in_maps.append(m)
    res = run_bass_kernel_spmd(nc, in_maps, core_ids=list(range(8)))
    out = np.empty((B, S, D), dtype=np.float32)
    for core in range(8):
        b, h = core // 2, core % 2
        out[b, h * QR:(h + 1) * QR] = res.results[core]["out"]
    return out


# version 09
# speedup vs baseline: 3.1231x; 1.1499x over previous
import numpy as np
import concourse.bass as bass
import concourse.mybir as mybir
from concourse.bass_utils import run_bass_kernel_spmd

F32 = mybir.dt.float32
BF16 = mybir.dt.bfloat16
AF = mybir.ActivationFunctionType
ALU = mybir.AluOpType
AX = mybir.AxisListType

B, S, D = 4, 4096, 1024
SK = S + 1
QR = 2048
NT1 = 32
NG1 = 8
NG2 = 4
EPS = 1e-5


class Ch:
    __slots__ = ("name", "n", "sem")
    def __init__(self, name):
        self.name, self.n, self.sem = name, 0, None


class Ev:
    __slots__ = ("eng", "fn", "waits", "inc", "tag")
    def __init__(self, eng, fn, waits, inc, tag):
        self.eng, self.fn, self.waits, self.inc, self.tag = eng, fn, waits, inc, tag


class Prog:
    def __init__(self):
        self.events = []

    def emit(self, eng, fn, waits=(), inc=None, tag=""):
        waits = [w for w in waits if w is not None]
        tok = None
        if inc is not None:
            if isinstance(inc, Ch):
                inc = (inc, 16 if inc.name.startswith("d") else 1)
            ch, k = inc
            ch.n += k
            tok = (ch, ch.n)
        self.events.append(Ev(eng, fn, waits, inc, tag))
        return tok

    def check(self):
        import collections
        q = collections.defaultdict(collections.deque)
        for ev in self.events:
            q[ev.eng].append(ev)
        vals = collections.defaultdict(int)
        progress = True
        while progress:
            progress = False
            for eng in list(q):
                while q[eng]:
                    ev = q[eng][0]
                    if all(vals[c.name] >= v for c, v in ev.waits):
                        q[eng].popleft()
                        if ev.inc is not None:
                            vals[ev.inc[0].name] += ev.inc[1]
                        progress = True
                    else:
                        break
        stuck = {e: (d[0].tag, [(c.name, v, vals[c.name]) for c, v in d[0].waits])
                 for e, d in q.items() if d}
        if stuck:
            raise RuntimeError(f"schedule deadlock: {stuck}")

    def realize(self, block):
        import collections
        marks = collections.defaultdict(int)

        def body(eng_name):
            def run(eng):
                for ev in self.events:
                    if ev.eng != eng_name:
                        continue
                    for ch, v in ev.waits:
                        if marks[(eng_name, ch.name)] < v:
                            eng.wait_ge(ch.sem, v)
                            marks[(eng_name, ch.name)] = v
                    inst = ev.fn(eng)
                    if ev.inc is not None:
                        inst.then_inc(ev.inc[0].sem, ev.inc[1])
            return run
        block.sync(body("sync"))
        block.gpsimd(body("gpsimd"))
        block.tensor(body("tensor"))
        block.vector(body("vector"))
        block.scalar(body("scalar"))


def build_nc():
    import contextlib
    import ml_dtypes
    nc = bass.Bass("TRN2", target_bir_lowering=False)

    x_h = nc.dram_tensor("x", [S, D], F32, kind="ExternalInput").ap()
    wq_h = nc.dram_tensor("wq", [129, D], F32, kind="ExternalInput").ap()
    bq_h = nc.dram_tensor("bq", [129], F32, kind="ExternalInput").ap()
    wk_h = nc.dram_tensor("wk", [129, D], F32, kind="ExternalInput").ap()
    bk_h = nc.dram_tensor("bk", [129], F32, kind="ExternalInput").ap()
    wv_h = nc.dram_tensor("wv", [D, D], F32, kind="ExternalInput").ap()
    bv_h = nc.dram_tensor("bv", [D], F32, kind="ExternalInput").ap()
    k0_h = nc.dram_tensor("k0", [129], F32, kind="ExternalInput").ap()
    v0_h = nc.dram_tensor("v0", [D], F32, kind="ExternalInput").ap()
    U1_h = nc.dram_tensor("U1", [D, 256], F32, kind="ExternalInput").ap()
    V1_h = nc.dram_tensor("V1", [256, 4096], F32, kind="ExternalInput").ap()
    b1_h = nc.dram_tensor("b1", [4096], F32, kind="ExternalInput").ap()
    U2_h = nc.dram_tensor("U2", [4096, 256], F32, kind="ExternalInput").ap()
    V2_h = nc.dram_tensor("V2", [256, D], F32, kind="ExternalInput").ap()
    b2_h = nc.dram_tensor("b2", [D], F32, kind="ExternalInput").ap()
    out_h = nc.dram_tensor("out", [QR, D], F32, kind="ExternalOutput").ap()
    ident_h = nc.inline_tensor(np.eye(128, dtype=ml_dtypes.bfloat16), "identity")

    ctx = contextlib.ExitStack()
    sb = lambda name, shape, dt: ctx.enter_context(nc.sbuf_tensor(name, shape, dt))
    psu = lambda name, shape, dt: ctx.enter_context(nc.psum_tensor(name, shape, dt))

    with ctx:
        XN = sb("XN", [128, NT1 * D], BF16)
        K_hi = sb("K_hi", [128, SK], BF16)
        K_lo = sb("K_lo", [1, SK], BF16)
        wqT = sb("wqT", [128, 8 * 129], BF16)
        wkT = sb("wkT", [128, 8 * 129], BF16)
        wvT = sb("wvT", [128, 8 * D], BF16)
        U1s = sb("U1s", [128, 8 * 256], BF16)
        V1s = sb("V1s", [128, 2 * 4096], BF16)
        U2s = sb("U2s", [128, 32 * 256], BF16)
        V2s = sb("V2s", [128, 2 * D], BF16)
        bvv0 = sb("bvv0", [2, D], BF16)
        b2row = sb("b2row", [1, D], BF16)
        b1T = sb("b1T", [128, 32], F32)
        bqT = sb("bqT", [128, 1], F32)
        bqL = sb("bqL", [1, 1], F32)
        bkT = sb("bkT", [128, 1], F32)
        bkL = sb("bkL", [1, 1], F32)
        ident = sb("ident", [128, 128], BF16)
        ones11 = sb("ones11", [1, 1], BF16)
        onescol = sb("onescol", [1, 128], BF16)
        onesc1 = sb("onesc1", [128, 1], BF16)
        scr = sb("scr", [128, 4100], F32)
        P_sb = sb("P_sb", [128, SK], BF16)
        PT = sb("PT", [128, 2048], BF16)
        xnT = sb("xnT", [128, 8 * 512], BF16)
        acc = sb("acc", [128, 4 * D], F32)
        xb = [sb("xb0", [128, D], F32), sb("xb1", [128, D], F32)]
        q_sb = sb("q_sb", [128, 128], BF16)
        qlo = sb("qlo", [1, 128], BF16)
        cs_bf = sb("cs_bf", [128, 2], BF16)
        csT = sb("csT", [2, 128], BF16)
        sum1 = sb("sum1", [128, 32], F32)
        sumsq = sb("sumsq", [128, 32], F32)
        mean = sb("mean", [128, 32], F32)
        msq = sb("msq", [128, 32], F32)
        varv = sb("varv", [128, 32], F32)
        lnv = sb("lnv", [128, 32], F32)
        rstd = sb("rstd", [128, 32], F32)
        nmrs = sb("nmrs", [128, 32], F32)
        maxp = sb("maxp", [128, 10], F32)
        negm = sb("negm", [128, 1], F32)
        Zf = sb("Zf", [128, 1], F32)
        rz = sb("rz", [128, 1], F32)
        spacer = sb("spacer", [128, 64], F32)
        sspace = sb("sspace", [128, 1], F32)
        k0f = sb("k0f", [128, 1], F32)
        k0l = sb("k0l", [1, 1], F32)
        eps_c = sb("eps_c", [128, 1], F32)
        nhalf_c = sb("nhalf_c", [128, 1], F32)

        xn_f32 = scr[:, 0:1024]
        Pf = scr[:, 1024:1536].bitcast(BF16)
        H2 = scr[:, 1536:2048].bitcast(BF16)
        Qf = scr[:, 2048:2560].bitcast(BF16)
        junkS = scr[:, 2560:3072].bitcast(BF16)
        junkV = scr[:, 3072:3584].bitcast(BF16)
        U_bf = P_sb[:, 0:1024]
        UT = P_sb[:, 1024:2048]
        lo_bf = P_sb[0:1, 2048:3072]
        hi_bf = P_sb[:, 0:1024]

        pA = psu("pA", [128, 1024], F32)
        pB = psu("pB", [128, 1024], F32)
        pC = psu("pC", [128, 1024], F32)
        pD = psu("pD", [128, 1024], F32)
        pB_bf = pB[:].bitcast(BF16)
        pC_bf = pC[:].bitcast(BF16)

        P = Prog()
        dX = [Ch("dX0"), Ch("dX1")]
        dW = [Ch("dW0"), Ch("dW1")]
        dWs, dO = Ch("dWs"), Ch("dO")
        V, SC, T = Ch("V"), Ch("S"), Ch("T")
        for c in (dX[0], dX[1], dW[0], dW[1], dWs, dO, V, SC, T):
            c.sem = ctx.enter_context(nc.semaphore(c.name))

        bank_free = {i: None for i in range(8)}
        em = P.emit

        def em2dma(eng, fn, waits=(), inc=None, tag=""):
            em(eng, fn, waits=waits, inc=inc, tag=tag)
            return em(eng, fn, inc=inc, tag=tag + "2")

        def vspace(tag="vsp"):
            em("vector", lambda e: e.memset(spacer[:], 0.0), tag=tag)

        em("vector", lambda e: e.memset(sspace[:], 1.0), tag="pre0")
        t_pre = em("vector", lambda e: e.memset(spacer[:], 1.0), inc=V, tag="pre1")
        em("scalar", lambda e: e.activation(sspace[:], sspace[:], AF.Copy),
           waits=[t_pre], inc=SC, tag="actgate")
        em("scalar", lambda e: e.activation(sspace[:], sspace[:], AF.Exp),
           inc=SC, tag="actgate2")
        t_actgate = (SC, SC.n)
        em("vector", lambda e: e.reciprocal(spacer[:, 0:1], spacer[:, 0:1]),
           waits=[t_actgate], inc=V, tag="dvegate")
        t_dvegate = (V, V.n)
        for eng in ("sync", "gpsimd", "tensor"):
            em(eng, lambda e: e.nop(), waits=[t_actgate, t_dvegate], tag="bootw")
        em("gpsimd", lambda e: e.dma_start(out=xb[0][:, 0:512], in_=x_h[0:128, 0:512]),
           inc=dW[0], tag="boot")
        em("gpsimd", lambda e: e.dma_start(out=xb[1][:, 0:512], in_=x_h[0:128, 0:512]),
           inc=dW[1], tag="boot")
        em("gpsimd", lambda e: e.dma_start(out=spacer[:], in_=x_h[0:128, 0:64]),
           inc=dWs, tag="boot")
        em("sync", lambda e: e.dma_start(out=xb[0][:, 512:1024], in_=x_h[0:128, 512:1024]),
           inc=dX[0], tag="boot")
        em("sync", lambda e: e.dma_start(out=xb[1][:, 512:1024], in_=x_h[0:128, 512:1024]),
           inc=dX[1], tag="boot")
        t_bw = [(dW[0], dW[0].n), (dW[1], dW[1].n), (dWs, dWs.n),
                (dX[0], dX[0].n), (dX[1], dX[1].n)]
        em("vector", lambda e: e.reduce_sum(sum1[:, 0:1], xb[0][:], axis=AX.X),
           waits=t_bw, inc=V, tag="bootrd")
        em("vector", lambda e: e.reduce_sum(sum1[:, 1:2], xb[1][:], axis=AX.X),
           inc=V, tag="bootrd")
        t_boot = (V, V.n)

        em2dma("gpsimd", lambda e: e.dma_start(out=ident[:], in_=ident_h.ap()),
           waits=[t_boot], inc=dWs)
        em2dma("gpsimd", lambda e: e.dma_start(
            out=bqT[:], in_=bq_h[0:128].rearrange("(p o) -> p o", o=1),
            allow_slow_non_contiguous=True), inc=dWs)
        em2dma("gpsimd", lambda e: e.dma_start(
            out=bqL[:], in_=bq_h[128:129].rearrange("(p o) -> p o", o=1)), inc=dWs)
        em2dma("gpsimd", lambda e: e.dma_start(
            out=bkT[:], in_=bk_h[0:128].rearrange("(p o) -> p o", o=1),
            allow_slow_non_contiguous=True), inc=dWs)
        em2dma("gpsimd", lambda e: e.dma_start(
            out=bkL[:], in_=bk_h[128:129].rearrange("(p o) -> p o", o=1)), inc=dWs)
        em2dma("gpsimd", lambda e: e.dma_start(
            out=b1T[:], in_=b1_h.rearrange("(c p) -> p c", p=128),
            allow_slow_non_contiguous=True), inc=dWs)
        em2dma("gpsimd", lambda e: e.dma_start(
            out=k0f[:], in_=k0_h[0:128].rearrange("(p o) -> p o", o=1),
            allow_slow_non_contiguous=True), inc=dWs)
        em2dma("gpsimd", lambda e: e.dma_start(
            out=k0l[:], in_=k0_h[128:129].rearrange("(p o) -> p o", o=1)), inc=dWs)
        em2dma("gpsimd", lambda e: e.dma_start(
            out=b2row[0:1, :], in_=b2_h.rearrange("(o d) -> o d", o=1)), inc=dWs)
        em2dma("gpsimd", lambda e: e.dma_start(
            out=bvv0[0:1, :], in_=bv_h.rearrange("(o d) -> o d", o=1)), inc=dWs)
        em2dma("gpsimd", lambda e: e.dma_start(
            out=bvv0[1:2, :], in_=v0_h.rearrange("(o d) -> o d", o=1)), inc=dWs)
        t_smalls = (dWs, dWs.n)
        t_ident = t_smalls
        t_bias = t_smalls
        t_wrow = t_smalls
        em("vector", lambda e: e.memset(ones11[:], 1.0), inc=V)
        em("vector", lambda e: e.memset(onescol[:], 1.0), inc=V)
        em("vector", lambda e: e.memset(onesc1[:], 1.0), inc=V)
        t_ones = (V, V.n)
        em("vector", lambda e: e.memset(eps_c[:], EPS), inc=V)
        em("vector", lambda e: e.memset(nhalf_c[:], -0.5), inc=V)
        t_consts = (V, V.n)
        em("vector", lambda e: e.tensor_copy(K_hi[:, 4096:4097], k0f[:]),
           waits=[t_smalls], inc=V)
        em("vector", lambda e: e.tensor_copy(K_lo[0:1, 4096:4097], k0l[:]), inc=V)
        t_k0 = (V, V.n)

        stage_tok = [t_boot, t_boot]
        load_tok = [None, None]
        stage_i = [0]

        def stage_load(src_ap, rows=128, cols=D):
            i = stage_i[0] & 1
            stage_i[0] += 1
            dst = xb[i][0:rows, 0:cols]
            tk = em2dma("gpsimd", lambda e, d=dst, s=src_ap: e.dma_start(out=d, in_=s),
                    waits=[stage_tok[i]], inc=dW[i], tag="wload")
            load_tok[i] = tk
            return i, dst

        def load_cast(src_ap, dst_bf, rows=128, cols=D, extra=None):
            i, st = stage_load(src_ap, rows, cols)
            tk = em("vector", lambda e, d=dst_bf, s=st: e.tensor_copy(d, s),
                    waits=[load_tok[i], extra], inc=V, tag="wcast")
            stage_tok[i] = tk
            return tk

        for c in range(8):
            load_cast(U1_h[c * 128:(c + 1) * 128, :], U1s[:, c * 256:(c + 1) * 256], cols=256)
        for m in range(2):
            for pce in range(4):
                load_cast(V1_h[m * 128:(m + 1) * 128, pce * 1024:(pce + 1) * 1024],
                          V1s[:, m * 4096 + pce * 1024: m * 4096 + (pce + 1) * 1024])
        for c in range(32):
            load_cast(U2_h[c * 128:(c + 1) * 128, :], U2s[:, c * 256:(c + 1) * 256], cols=256)
        for m in range(2):
            load_cast(V2_h[m * 128:(m + 1) * 128, :], V2s[:, m * D:(m + 1) * D])

        def prep_qk(w_h, wT):
            tcast = load_cast(w_h[0:128, :], hi_bf, extra=bank_free[4])
            for c in range(8):
                em("tensor", lambda e, c=c: e.transpose(
                    pC_bf[:, c * 128:(c + 1) * 128], hi_bf[:, c * 128:(c + 1) * 128], ident[:]),
                    waits=[tcast, t_ident, bank_free[4] if c == 0 else None],
                    inc=(T, 1) if c == 7 else None, tag="qkT")
            t_tr = (T, T.n)
            wT3 = wT[:].rearrange("p (c k) -> p c k", k=129)
            pc3 = pC_bf[:, 0:1024].rearrange("p (c k) -> p c k", k=128)
            ev = em("vector", lambda e, w3=wT3, p3=pc3: e.tensor_copy(w3[:, :, 0:128], p3),
                    waits=[t_tr], inc=V, tag="qkTev")
            bank_free[4] = ev
            tc2 = load_cast(w_h[128:129, :], lo_bf[0:1, :], rows=1, extra=bank_free[6])
            for c in range(8):
                em("tensor", lambda e, c=c: e.matmul(
                    pD[:, c:c + 1], lo_bf[0:1, c * 128:(c + 1) * 128], ones11[:],
                    start=True, stop=True),
                    waits=[tc2, t_ones, bank_free[6] if c == 0 else None],
                    inc=(T, 1) if c == 7 else None, tag="qklo")
            t_lo = (T, T.n)
            ev2 = em("vector", lambda e, w3=wT3: e.tensor_copy(
                w3[:, :, 128:129], pD[:, 0:8].rearrange("p (c k) -> p c k", k=1)),
                waits=[t_lo], inc=V, tag="qkloev")
            bank_free[6] = ev2

        prep_qk(wq_h, wqT)
        prep_qk(wk_h, wkT)

        wv3 = wvT[:].rearrange("p (dc e) -> p dc e", e=1024)
        for ec in range(8):
            tcast = load_cast(wv_h[ec * 128:(ec + 1) * 128, :], hi_bf, extra=bank_free[4])
            for c in range(8):
                em("tensor", lambda e, c=c: e.transpose(
                    pC_bf[:, c * 128:(c + 1) * 128], hi_bf[:, c * 128:(c + 1) * 128], ident[:]),
                    waits=[tcast, bank_free[4] if c == 0 else None],
                    inc=(T, 1) if c == 7 else None, tag="wvT")
            t_tr = (T, T.n)
            ev = em("vector", lambda e, ec=ec: e.tensor_copy(
                wv3[:, :, ec * 128:(ec + 1) * 128],
                pC_bf[:, 0:1024].rearrange("p (c k) -> p c k", k=128)),
                waits=[t_tr], inc=V, tag="wvTev")
            bank_free[4] = ev

        def ln_stats(x_t, waits, t):
            em("vector", lambda e, t=t: e.scalar_tensor_tensor(
                junkV, x_t, 1.0, x_t, op0=ALU.mult, op1=ALU.mult,
                accum_out=sumsq[:, t:t + 1]),
                waits=waits, inc=V, tag="lnsq")
            tk = em("vector", lambda e, t=t: e.reduce_sum(
                sum1[:, t:t + 1], x_t, axis=AX.X), inc=V, tag="lnsum")
            return tk

        def vgap(n=16):
            for _ in range(n):
                em("vector", lambda e: e.memset(junkV, 0.0), tag="vgap")

        def sgap(n=5):
            for _ in range(n):
                em("scalar", lambda e: e.activation(sspace[:], sspace[:], AF.Copy),
                   tag="sgap")

        def ln_chain_batched(t_stats_done):
            em("vector", lambda e: e.tensor_scalar_mul(mean[:], sum1[:], 1.0 / D),
               waits=[t_stats_done], tag="lnmean")
            em("vector", lambda e: e.tensor_scalar_mul(msq[:], sumsq[:], 1.0 / D),
               tag="lnmsq")
            vgap()
            em("vector", lambda e: e.tensor_mul(varv[:], mean[:], mean[:]), tag="lnm2")
            vgap()
            tv2 = em("vector", lambda e: e.tensor_sub(varv[:], msq[:], varv[:]),
                     inc=V, tag="lnvar")
            sgap()
            em("scalar", lambda e: e.activation(lnv[:], varv[:], AF.Ln, bias=eps_c[:]),
               waits=[tv2, t_consts], tag="lnln")
            sgap()
            em("scalar", lambda e: e.activation(rstd[:], lnv[:], AF.Exp, scale=nhalf_c[:]),
               tag="lnexp")
            tr = em("scalar", lambda e: e.activation(sspace[:], sspace[:], AF.Copy),
                    inc=SC, tag="lnexp_m")
            vgap()
            tn = em("vector", lambda e: e.scalar_tensor_tensor(
                nmrs[:], mean[:], -1.0, rstd[:], op0=ALU.mult, op1=ALU.mult),
                waits=[tr], inc=V, tag="lnnm")
            tnm = em("vector", lambda e: e.memset(junkV, 0.0), inc=V, tag="lnnm_m")
            return tnm

        def ln_affine(x_t, waits, t, out_ap):
            return em("scalar", lambda e, o=out_ap, t=t: e.activation(
                o, x_t, AF.Identity, bias=nmrs[:, t:t + 1], scale=rstd[:, t:t + 1]),
                waits=waits, inc=SC, tag="lnaff")

        def xnt_group(tiles, src_toks):
            for c in range(8):
                bk = 4 + (c & 1)
                off = (c & 1) * 1024
                for j, tl in enumerate(tiles):
                    em("tensor", lambda e, c=c, tl=tl, off=off, j=j: e.transpose(
                        pC_bf[:, off + j * 128: off + (j + 1) * 128],
                        XN[:, tl * D + c * 128: tl * D + (c + 1) * 128], ident[:]),
                        waits=[src_toks[j], bank_free[bk] if j == 0 else None],
                        inc=(T, 1) if j == 3 else None, tag="xnT")
                t_tr = (T, T.n)
                ev = em("vector", lambda e, c=c, off=off: e.tensor_copy(
                    xnT[:, c * 512:(c + 1) * 512], pC_bf[:, off:off + 512]),
                    waits=[t_tr], inc=V, tag="xnTev")
                bank_free[bk] = ev
            return (V, V.n)

        xb_tok = [stage_tok[0], stage_tok[1]]
        xn_tok = [None] * NT1
        xload_tok = [None] * (NT1 + 1)
        for t in range(NT1):
            i = t & 1
            xload_tok[t] = em2dma("sync", lambda e, t=t, i=i: e.dma_start(
                out=xb[i][:], in_=x_h[t * 128:(t + 1) * 128, :]),
                waits=[xb_tok[i]], inc=dX[i], tag="xloadA")
            if t > 0:
                xb_tok[(t - 1) & 1] = ln_stats(
                    xb[(t - 1) & 1][:], [xload_tok[t - 1], xload_tok[t]], t - 1)
        xload_tok[NT1] = em2dma("sync", lambda e: e.dma_start(
            out=xb[0][:], in_=x_h[0:128, :]), waits=[xb_tok[0]], inc=dX[0],
            tag="xloadAd")
        xb_tok[0] = xload_tok[NT1]
        xb_tok[1] = ln_stats(xb[1][:], [xload_tok[NT1 - 1], xload_tok[NT1]], NT1 - 1)
        t_chain = ln_chain_batched(xb_tok[1])
        def kproj_group(g):
            t_xnt = xnt_group([g * 4 + t for t in range(4)],
                              [xn_tok[g * 4 + t] for t in range(4)])
            for c in range(8):
                em("tensor", lambda e, c=c: e.matmul(
                    pA[:, 0:512], wkT[:, c * 129:c * 129 + 128],
                    xnT[:, c * 512:(c + 1) * 512],
                    start=(c == 0), stop=(c == 7)),
                    waits=[t_xnt, bank_free[0] if c == 0 else None], tag="kprojhi")
                em("tensor", lambda e, c=c: e.matmul(
                    pA[0:1, 512:1024], wkT[:, c * 129 + 128:c * 129 + 129],
                    xnT[:, c * 512:(c + 1) * 512],
                    start=(c == 0), stop=(c == 7)),
                    waits=[bank_free[1] if c == 0 else None],
                    inc=(T, 1) if c == 7 else None, tag="kprojlo")
            t_k = (T, T.n)
            e1 = em("scalar", lambda e, g=g: e.activation(
                K_hi[:, g * 512:(g + 1) * 512], pA[:, 0:512], AF.Identity, bias=bkT[:]),
                waits=[t_k, t_bias], inc=SC, tag="kevhi")
            e2 = em("scalar", lambda e, g=g: e.activation(
                K_lo[0:1, g * 512:(g + 1) * 512], pA[0:1, 512:1024], AF.Identity,
                bias=bkL[:]),
                inc=SC, tag="kevlo")
            bank_free[0] = e1
            bank_free[1] = e2

        for idx in range(NT1 + 1):
            i = idx & 1
            tsrc = 0 if idx == NT1 else idx
            xload_tok[idx] = em2dma("sync", lambda e, tsrc=tsrc, i=i: e.dma_start(
                out=xb[i][:], in_=x_h[tsrc * 128:(tsrc + 1) * 128, :]),
                waits=[xb_tok[i]], inc=dX[i], tag="xloadB")
            if idx == NT1:
                xb_tok[0] = xload_tok[idx]
            if idx > 0:
                tile = idx - 1
                pt = tile & 1
                ta = ln_affine(xb[pt][:], [xload_tok[tile], xload_tok[idx], t_chain],
                               tile, XN[:, tile * D:(tile + 1) * D])
                xb_tok[pt] = ta
                xn_tok[tile] = ta
            if idx > 0 and idx % 4 == 0:
                kproj_group(idx // 4 - 1)
        t_xn_done = xn_tok[NT1 - 1]
        t_k_done = (SC, SC.n)

        acc_free = None
        last_exp = [None]
        prev_tadd = [None]
        for g in range(NG2):
            acc_toks = [None] * 4
            for t in range(4):
                tile = g * 4 + t
                i = tile & 1
                xload_tok[tile] = em2dma("sync", lambda e, tile=tile, i=i: e.dma_start(
                    out=xb[i][:], in_=x_h[tile * 128:(tile + 1) * 128, :]),
                    waits=[xb_tok[i]], inc=dX[i], tag="xloadC")
                if t > 0:
                    tile_p = tile - 1
                    pt = tile_p & 1
                    ta = ln_affine(xb[pt][:],
                                   [xload_tok[tile_p], xload_tok[tile], t_chain,
                                    prev_tadd[0]],
                                   tile_p, xn_f32)
                    xn_tok[tile_p] = ta
                    tadd = em("vector", lambda e, t=t, pt=pt: e.tensor_add(
                        acc[:, (t - 1) * D:t * D], xb[pt][:], xn_f32),
                        waits=[ta, acc_free], inc=V, tag="accinit")
                    xb_tok[pt] = tadd
                    acc_toks[t - 1] = tadd
                    prev_tadd[0] = tadd
            tile_p = g * 4 + 3
            pt = tile_p & 1
            idum = 1 - pt
            tdum = em2dma("sync", lambda e, idum=idum: e.dma_start(
                out=xb[idum][:], in_=x_h[0:128, :]),
                waits=[xb_tok[idum]], inc=dX[idum], tag="xloadCd")
            xb_tok[idum] = tdum
            ta = ln_affine(xb[pt][:], [xload_tok[tile_p], tdum, t_chain,
                                       prev_tadd[0]],
                           tile_p, xn_f32)
            xn_tok[tile_p] = ta
            tadd = em("vector", lambda e, pt=pt: e.tensor_add(
                acc[:, 3 * D:4 * D], xb[pt][:], xn_f32),
                waits=[ta, acc_free], inc=V, tag="accinit")
            xb_tok[pt] = tadd
            acc_toks[3] = tadd
            prev_tadd[0] = tadd
            t_xnt = xnt_group([g * 4 + t for t in range(4)],
                              [xn_tok[g * 4 + t] for t in range(4)])
            for c in range(8):
                for m in range(2):
                    em("tensor", lambda e, c=c, m=m: e.matmul(
                        pA[:, m * 512:(m + 1) * 512],
                        U1s[:, c * 256 + m * 128: c * 256 + (m + 1) * 128],
                        xnT[:, c * 512:(c + 1) * 512], start=(c == 0), stop=(c == 7)),
                        waits=[t_xnt if c == 0 else None,
                               bank_free[m] if c == 0 else None],
                        inc=(T, 1) if (c == 7 and m == 1) else None, tag="ffnP")
            t_p = (T, T.n)
            ep = em("scalar", lambda e: e.activation(Pf, pA[:], AF.Copy),
                    waits=[t_p], inc=SC, tag="ffnPev")
            bank_free[0] = ep
            bank_free[1] = ep
            gelu_tok = [None, None]
            for j in range(32):
                jb = j & 1
                em("tensor", lambda e, j=j, jb=jb: e.matmul(
                    pB[:, jb * 512:(jb + 1) * 512],
                    V1s[:, 0 * 4096 + j * 128:0 * 4096 + (j + 1) * 128],
                    Pf[:, 0:512], start=True, stop=False),
                    waits=[ep if j < 2 else None,
                           (gelu_tok[jb] if j >= 2 else bank_free[2 + jb])], tag="ffnH")
                em("tensor", lambda e, j=j, jb=jb: e.matmul(
                    pB[:, jb * 512:(jb + 1) * 512],
                    V1s[:, 1 * 4096 + j * 128:1 * 4096 + (j + 1) * 128],
                    Pf[:, 512:1024], start=False, stop=True),
                    inc=(T, 1), tag="ffnH2")
                t_h = (T, T.n)
                gt = em("scalar", lambda e, j=j, jb=jb: e.activation(
                    H2[:, jb * 512:(jb + 1) * 512], pB[:, jb * 512:(jb + 1) * 512],
                    AF.Gelu_apprx_tanh, bias=b1T[:, j:j + 1]),
                    waits=[t_h, t_smalls], inc=SC, tag="gelu")
                gelu_tok[jb] = gt
                for m in range(2):
                    em("tensor", lambda e, j=j, jb=jb, m=m: e.matmul(
                        pC[:, m * 512:(m + 1) * 512],
                        U2s[:, j * 256 + m * 128:j * 256 + (m + 1) * 128],
                        H2[:, jb * 512:(jb + 1) * 512], start=(j == 0), stop=(j == 31)),
                        waits=[gt, (bank_free[4 + m] if j == 0 else None)],
                        inc=(T, 1) if (j == 31 and m == 1) else None, tag="ffnQ")
            bank_free[2] = gelu_tok[0]
            bank_free[3] = gelu_tok[1]
            t_q = (T, T.n)
            eq = em("vector", lambda e: e.tensor_copy(Qf, pC[:]),
                    waits=[t_q], inc=V, tag="ffnQev")
            bank_free[4] = eq
            bank_free[5] = eq
            for t in range(4):
                for nh in range(2):
                    for m in range(2):
                        em("tensor", lambda e, t=t, nh=nh, m=m: e.matmul(
                            pD[:, nh * 512:(nh + 1) * 512],
                            Qf[:, m * 512 + t * 128: m * 512 + (t + 1) * 128],
                            V2s[:, m * D + nh * 512: m * D + (nh + 1) * 512],
                            start=(m == 0), stop=False),
                            waits=[eq if m == 0 else None,
                                   bank_free[6 + nh] if m == 0 else None], tag="ffnO")
                    em("tensor", lambda e, nh=nh: e.matmul(
                        pD[:, nh * 512:(nh + 1) * 512], onescol[:],
                        b2row[0:1, nh * 512:(nh + 1) * 512], start=False, stop=True),
                        waits=[t_ones, t_wrow],
                        inc=(T, 1) if nh == 1 else None, tag="ffnOb")
                t_o = (T, T.n)
                eo = em("vector", lambda e, t=t: e.tensor_add(
                    acc[:, t * D:(t + 1) * D], acc[:, t * D:(t + 1) * D], pD[:]),
                    waits=[t_o, acc_toks[t]], inc=V, tag="accO")
                bank_free[6] = eo
                bank_free[7] = eo
            for t in range(4):
                for c in range(8):
                    em("tensor", lambda e, c=c, t=t: e.matmul(
                        pA[:, 0:128], wqT[:, c * 129:c * 129 + 128],
                        xnT[:, c * 512 + t * 128: c * 512 + (t + 1) * 128],
                        start=(c == 0), stop=(c == 7)),
                        waits=[bank_free[0] if c == 0 else None], tag="qhi")
                    em("tensor", lambda e, c=c, t=t: e.matmul(
                        pA[0:1, 512:640], wqT[:, c * 129 + 128:c * 129 + 129],
                        xnT[:, c * 512 + t * 128: c * 512 + (t + 1) * 128],
                        start=(c == 0), stop=(c == 7)),
                        waits=[bank_free[1] if c == 0 else None],
                        inc=(T, 1) if c == 7 else None, tag="qlo")
                t_qp = (T, T.n)
                eqh = em("scalar", lambda e: e.activation(
                    q_sb[:], pA[:, 0:128], AF.Identity, bias=bqT[:]),
                    waits=[t_qp, t_bias], inc=SC, tag="qevhi")
                eql = em("scalar", lambda e: e.activation(
                    qlo[:], pA[0:1, 512:640], AF.Identity, bias=bqL[:]),
                    inc=SC, tag="qevlo")
                bank_free[0] = eqh
                slab_ev = [None] * 8
                for s_ in range(8):
                    sb_ = s_ & 1
                    em("tensor", lambda e, s_=s_, sb_=sb_: e.matmul(
                        pB[:, sb_ * 512:(sb_ + 1) * 512], q_sb[:],
                        K_hi[:, s_ * 512:(s_ + 1) * 512],
                        start=True, stop=False),
                        waits=[eqh,
                               t_k_done if s_ == 0 else None,
                               (slab_ev[s_ - 2] if s_ >= 2 else bank_free[2 + sb_])],
                        tag="schi")
                    em("tensor", lambda e, s_=s_, sb_=sb_: e.matmul(
                        pB[:, sb_ * 512:(sb_ + 1) * 512], qlo[:],
                        K_lo[0:1, s_ * 512:(s_ + 1) * 512],
                        start=False, stop=True),
                        waits=[eql], inc=(T, 1), tag="sclo")
                    t_sc = (T, T.n)
                    se = em("scalar", lambda e, s_=s_, sb_=sb_: e.activation(
                        scr[:, s_ * 512:(s_ + 1) * 512], pB[:, sb_ * 512:(sb_ + 1) * 512],
                        AF.Copy),
                        waits=[t_sc, acc_toks[3] if (t == 0 and s_ == 0) else None],
                        inc=SC, tag="scev")
                    slab_ev[s_] = se
                    em("vector", lambda e, s_=s_: e.reduce_max(
                        maxp[:, s_:s_ + 1], scr[:, s_ * 512:(s_ + 1) * 512], axis=AX.X),
                        waits=[se], tag="maxp")
                bank_free[2] = slab_ev[6]
                bank_free[3] = slab_ev[7]
                em("tensor", lambda e: e.matmul(
                    pA[:, 640:641], q_sb[:], K_hi[:, 4096:4097], start=True, stop=False),
                    waits=[eql, t_k0], tag="snkhi")
                em("tensor", lambda e: e.matmul(
                    pA[:, 640:641], qlo[:], K_lo[0:1, 4096:4097], start=False, stop=True),
                    inc=(T, 1), tag="snklo")
                t_snk = (T, T.n)
                es = em("scalar", lambda e: e.activation(
                    scr[:, 4096:4097], pA[:, 640:641], AF.Copy),
                    waits=[t_snk], inc=SC, tag="snkev")
                bank_free[1] = es
                em("vector", lambda e: e.tensor_copy(maxp[:, 8:9], scr[:, 4096:4097]),
                   waits=[es], tag="snkmax")
                vspace("vsp_negm")
                tm = em("vector", lambda e: e.reduce_max(
                    negm[:], maxp[:, 0:9], axis=AX.X, negate=True), inc=V, tag="negm")
                te = em("scalar", lambda e: e.activation(
                    P_sb[:, 0:SK], scr[:, 0:SK], AF.Exp, bias=negm[:]),
                    waits=[tm], inc=SC, tag="exp")
                last_exp[0] = te
                pt_ev = [None] * 8
                attnv_tok = [None] * 8
                for k in range(8):
                    bk = 4 + (k & 1)
                    off = (k & 1) * 1024
                    for j in range(4):
                        em("tensor", lambda e, k=k, j=j, off=off: e.transpose(
                            pC_bf[:, off + j * 128: off + (j + 1) * 128],
                            P_sb[:, (4 * k + j) * 128: (4 * k + j + 1) * 128], ident[:]),
                            waits=[te if (k == 0 and j == 0) else None,
                                   ((bank_free[bk] if k < 2 else pt_ev[k - 2])
                                    if j == 0 else None)],
                            inc=(T, 1) if j == 3 else None, tag="ptT")
                    t_pt = (T, T.n)
                    pe = em("vector", lambda e, k=k, off=off: e.tensor_copy(
                        PT[:, (k & 3) * 512:((k & 3) + 1) * 512], pC_bf[:, off:off + 512]),
                        waits=[t_pt, attnv_tok[k - 4] if k >= 4 else None],
                        inc=V, tag="ptev")
                    pt_ev[k] = pe
                    for cc in range(4):
                        c = 4 * k + cc
                        for nh in range(2):
                            em("tensor", lambda e, k=k, cc=cc, c=c, nh=nh: e.matmul(
                                pD[:, nh * 512:(nh + 1) * 512],
                                PT[:, (k & 3) * 512 + cc * 128: (k & 3) * 512 + (cc + 1) * 128],
                                XN[:, c * D + nh * 512: c * D + (nh + 1) * 512],
                                start=(c == 0), stop=(c == 31)),
                                waits=[pe if (cc == 0 and nh == 0) else None,
                                       t_xn_done if c == 0 else None,
                                       bank_free[6 + nh] if c == 0 else None],
                                tag="attnV")
                        em("tensor", lambda e, k=k, cc=cc, c=c: e.matmul(
                            pB[:, 512:513],
                            PT[:, (k & 3) * 512 + cc * 128: (k & 3) * 512 + (cc + 1) * 128],
                            onesc1[:], start=(c == 0), stop=(c == 31)),
                            waits=[(t_ones if c == 0 else None),
                                   (bank_free[3] if c == 0 else None)],
                            inc=(T, 1) if cc == 3 else None, tag="zr")
                    attnv_tok[k] = (T, T.n)
                t_u = attnv_tok[7]
                eu = em("vector", lambda e: e.tensor_copy(U_bf, pD[:]),
                        waits=[t_u], inc=V, tag="uev")
                bank_free[6] = eu
                bank_free[7] = eu
                em("vector", lambda e: e.tensor_copy(cs_bf[:, 0:1], pB[:, 512:513]),
                   waits=[t_u], tag="cs0")
                tcs = em("vector", lambda e: e.tensor_copy(cs_bf[:, 1:2], P_sb[:, 4096:4097]),
                         inc=V, tag="cs1")
                tct = em("tensor", lambda e: e.transpose(pB_bf[0:2, 0:128], cs_bf[:], ident[:]),
                         waits=[tcs, bank_free[2]], inc=(T, 1), tag="csT")
                tz = em("vector", lambda e: e.tensor_add(
                    Zf[:], pB[:, 512:513], P_sb[:, 4096:4097]), inc=V, tag="zadd")
                bank_free[3] = tz
                ecs = em("vector", lambda e: e.tensor_copy(csT[:], pB_bf[0:2, 0:128]),
                         waits=[tct], inc=V, tag="csTev")
                bank_free[2] = ecs
                em("vector", lambda e: e.reciprocal(rz[:], Zf[:]), tag="rz")
                ut_ev = [None, None]
                for kb in range(2):
                    off = kb * 1024
                    for j in range(4):
                        em("tensor", lambda e, kb=kb, j=j, off=off: e.transpose(
                            pC_bf[:, off + j * 128: off + (j + 1) * 128],
                            U_bf[:, (4 * kb + j) * 128: (4 * kb + j + 1) * 128], ident[:]),
                            waits=[eu if j == 0 else None,
                                   pt_ev[6 + kb] if j == 0 else None],
                            inc=(T, 1) if j == 3 else None, tag="utT")
                    t_ut = (T, T.n)
                    ue = em("vector", lambda e, kb=kb, off=off: e.tensor_copy(
                        UT[:, kb * 512:(kb + 1) * 512], pC_bf[:, off:off + 512]),
                        waits=[t_ut], inc=V, tag="utev")
                    ut_ev[kb] = ue
                bank_free[4] = ut_ev[0]
                bank_free[5] = ut_ev[1]
                for c in range(8):
                    for nh in range(2):
                        em("tensor", lambda e, c=c, nh=nh: e.matmul(
                            pA[:, nh * 512:(nh + 1) * 512], UT[:, c * 128:(c + 1) * 128],
                            wvT[:, c * D + nh * 512: c * D + (nh + 1) * 512],
                            start=(c == 0), stop=False),
                            waits=[ut_ev[c // 4] if nh == 0 else None,
                                   bank_free[nh] if c == 0 else None,
                                   ecs if c == 0 else None], tag="Y")
                for nh in range(2):
                    em("tensor", lambda e, nh=nh: e.matmul(
                        pA[:, nh * 512:(nh + 1) * 512], csT[:],
                        bvv0[:, nh * 512:(nh + 1) * 512], start=False, stop=True),
                        waits=[ecs, t_wrow], inc=(T, 1) if nh == 1 else None, tag="Yr2")
                t_y = (T, T.n)
                ey = em("vector", lambda e, t=t: e.scalar_tensor_tensor(
                    acc[:, t * D:(t + 1) * D], pA[:], rz[:], acc[:, t * D:(t + 1) * D],
                    op0=ALU.mult, op1=ALU.add),
                    waits=[t_y], inc=V, tag="accY")
                bank_free[0] = ey
                bank_free[1] = ey
            so = em("gpsimd", lambda e, g=g: e.dma_start(
                out=out_h[g * 512:(g + 1) * 512, :].rearrange("(t p) d -> p t d", p=128),
                in_=acc[:].rearrange("p (t d) -> p t d", t=4)),
                waits=[(V, V.n)], inc=dO, tag="store")
            acc_free = so
        em("gpsimd", lambda e: e.engine_nop(), waits=[(dO, dO.n)], tag="fin")

        P.check()
        with nc.Block() as block:
            P.realize(block)
    return nc


_NC_CACHE = [None]


def kernel(L0_attn_x, L0_attn_w_q, L0_attn_b_q, L0_attn_w_k, L0_attn_b_k,
           L0_attn_w_v, L0_attn_b_v, L0_attn_k_0, L0_attn_v_0,
           L0_ffn_U1, L0_ffn_V1, L0_ffn_b1, L0_ffn_U2, L0_ffn_V2, L0_ffn_b2):
    x = np.ascontiguousarray(np.asarray(L0_attn_x, dtype=np.float32))
    w = {
        "wq": np.ascontiguousarray(L0_attn_w_q, dtype=np.float32),
        "bq": np.ascontiguousarray(L0_attn_b_q, dtype=np.float32),
        "wk": np.ascontiguousarray(L0_attn_w_k, dtype=np.float32),
        "bk": np.ascontiguousarray(L0_attn_b_k, dtype=np.float32),
        "wv": np.ascontiguousarray(L0_attn_w_v, dtype=np.float32),
        "bv": np.ascontiguousarray(L0_attn_b_v, dtype=np.float32),
        "k0": np.ascontiguousarray(L0_attn_k_0, dtype=np.float32),
        "v0": np.ascontiguousarray(L0_attn_v_0, dtype=np.float32),
        "U1": np.ascontiguousarray(L0_ffn_U1, dtype=np.float32),
        "V1": np.ascontiguousarray(L0_ffn_V1, dtype=np.float32),
        "b1": np.ascontiguousarray(L0_ffn_b1, dtype=np.float32),
        "U2": np.ascontiguousarray(L0_ffn_U2, dtype=np.float32),
        "V2": np.ascontiguousarray(L0_ffn_V2, dtype=np.float32),
        "b2": np.ascontiguousarray(L0_ffn_b2, dtype=np.float32),
    }
    if _NC_CACHE[0] is None:
        _NC_CACHE[0] = build_nc()
    nc = _NC_CACHE[0]
    in_maps = []
    for core in range(8):
        b, h = core // 2, core % 2
        xb_ = x[b]
        xr = np.concatenate([xb_[h * QR:(h + 1) * QR], xb_[(1 - h) * QR:(2 - h) * QR]], axis=0)
        m = {"x": np.ascontiguousarray(xr)}
        m.update(w)
        in_maps.append(m)
    res = run_bass_kernel_spmd(nc, in_maps, core_ids=list(range(8)))
    out = np.empty((B, S, D), dtype=np.float32)
    for core in range(8):
        b, h = core // 2, core % 2
        out[b, h * QR:(h + 1) * QR] = res.results[core]["out"]
    return out


# version 11
# speedup vs baseline: 3.3945x; 1.0869x over previous
import numpy as np
import concourse.bass as bass
import concourse.mybir as mybir
from concourse.bass_utils import run_bass_kernel_spmd

F32 = mybir.dt.float32
BF16 = mybir.dt.bfloat16
AF = mybir.ActivationFunctionType
ALU = mybir.AluOpType
AX = mybir.AxisListType

B, S, D = 4, 4096, 1024
SK = S + 1
QR = 2048
NT1 = 32
NG1 = 8
NG2 = 4
EPS = 1e-5


class Ch:
    __slots__ = ("name", "n", "sem")
    def __init__(self, name):
        self.name, self.n, self.sem = name, 0, None


class Ev:
    __slots__ = ("eng", "fn", "waits", "inc", "tag")
    def __init__(self, eng, fn, waits, inc, tag):
        self.eng, self.fn, self.waits, self.inc, self.tag = eng, fn, waits, inc, tag


class Prog:
    def __init__(self):
        self.events = []

    def emit(self, eng, fn, waits=(), inc=None, tag=""):
        waits = [w for w in waits if w is not None]
        tok = None
        if inc is not None:
            if isinstance(inc, Ch):
                inc = (inc, 16 if inc.name.startswith("d") else 1)
            ch, k = inc
            ch.n += k
            tok = (ch, ch.n)
        self.events.append(Ev(eng, fn, waits, inc, tag))
        return tok

    def check(self):
        import collections
        q = collections.defaultdict(collections.deque)
        for ev in self.events:
            q[ev.eng].append(ev)
        vals = collections.defaultdict(int)
        progress = True
        while progress:
            progress = False
            for eng in list(q):
                while q[eng]:
                    ev = q[eng][0]
                    if all(vals[c.name] >= v for c, v in ev.waits):
                        q[eng].popleft()
                        if ev.inc is not None:
                            vals[ev.inc[0].name] += ev.inc[1]
                        progress = True
                    else:
                        break
        stuck = {e: (d[0].tag, [(c.name, v, vals[c.name]) for c, v in d[0].waits])
                 for e, d in q.items() if d}
        if stuck:
            raise RuntimeError(f"schedule deadlock: {stuck}")

    def realize(self, block):
        import collections
        marks = collections.defaultdict(int)

        def body(eng_name):
            def run(eng):
                for ev in self.events:
                    if ev.eng != eng_name:
                        continue
                    for ch, v in ev.waits:
                        if marks[(eng_name, ch.name)] < v:
                            eng.wait_ge(ch.sem, v)
                            marks[(eng_name, ch.name)] = v
                    inst = ev.fn(eng)
                    if ev.inc is not None:
                        inst.then_inc(ev.inc[0].sem, ev.inc[1])
            return run
        block.sync(body("sync"))
        block.gpsimd(body("gpsimd"))
        block.tensor(body("tensor"))
        block.vector(body("vector"))
        block.scalar(body("scalar"))


def build_nc():
    import contextlib
    import ml_dtypes
    nc = bass.Bass("TRN2", target_bir_lowering=False)

    x_h = nc.dram_tensor("x", [S, D], F32, kind="ExternalInput").ap()
    wq_h = nc.dram_tensor("wq", [129, D], F32, kind="ExternalInput").ap()
    bq_h = nc.dram_tensor("bq", [129], F32, kind="ExternalInput").ap()
    wk_h = nc.dram_tensor("wk", [129, D], F32, kind="ExternalInput").ap()
    bk_h = nc.dram_tensor("bk", [129], F32, kind="ExternalInput").ap()
    wv_h = nc.dram_tensor("wv", [D, D], F32, kind="ExternalInput").ap()
    bv_h = nc.dram_tensor("bv", [D], F32, kind="ExternalInput").ap()
    k0_h = nc.dram_tensor("k0", [129], F32, kind="ExternalInput").ap()
    v0_h = nc.dram_tensor("v0", [D], F32, kind="ExternalInput").ap()
    U1_h = nc.dram_tensor("U1", [D, 256], F32, kind="ExternalInput").ap()
    V1_h = nc.dram_tensor("V1", [256, 4096], F32, kind="ExternalInput").ap()
    b1_h = nc.dram_tensor("b1", [4096], F32, kind="ExternalInput").ap()
    U2_h = nc.dram_tensor("U2", [4096, 256], F32, kind="ExternalInput").ap()
    V2_h = nc.dram_tensor("V2", [256, D], F32, kind="ExternalInput").ap()
    b2_h = nc.dram_tensor("b2", [D], F32, kind="ExternalInput").ap()
    out_h = nc.dram_tensor("out", [QR, D], F32, kind="ExternalOutput").ap()
    ident_h = nc.inline_tensor(np.eye(128, dtype=ml_dtypes.bfloat16), "identity")

    ctx = contextlib.ExitStack()
    sb = lambda name, shape, dt: ctx.enter_context(nc.sbuf_tensor(name, shape, dt))
    psu = lambda name, shape, dt: ctx.enter_context(nc.psum_tensor(name, shape, dt))

    with ctx:
        XN = sb("XN", [128, NT1 * D], BF16)
        K_hi = sb("K_hi", [128, SK], BF16)
        K_lo = sb("K_lo", [1, SK], BF16)
        wqT = sb("wqT", [128, 8 * 129], BF16)
        wkT = sb("wkT", [128, 8 * 129], BF16)
        wvT = sb("wvT", [128, 8 * D], BF16)
        U1s = sb("U1s", [128, 8 * 256], BF16)
        V1s = sb("V1s", [128, 2 * 4096], BF16)
        U2s = sb("U2s", [128, 32 * 256], BF16)
        V2s = sb("V2s", [128, 2 * D], BF16)
        bvv0 = sb("bvv0", [2, D], BF16)
        b2row = sb("b2row", [1, D], BF16)
        b1T = sb("b1T", [128, 32], F32)
        bqT = sb("bqT", [128, 1], F32)
        bqL = sb("bqL", [1, 1], F32)
        bkT = sb("bkT", [128, 1], F32)
        bkL = sb("bkL", [1, 1], F32)
        ident = sb("ident", [128, 128], BF16)
        ones11 = sb("ones11", [1, 1], BF16)
        onescol = sb("onescol", [1, 128], BF16)
        onesc1 = sb("onesc1", [128, 1], BF16)
        scr = sb("scr", [128, 4100], F32)
        P_sb = sb("P_sb", [128, SK], BF16)
        PT = sb("PT", [128, 2048], BF16)
        xnT = sb("xnT", [128, 8 * 512], BF16)
        acc = sb("acc", [128, 4 * D], F32)
        xb = [sb("xb0", [128, D], F32), sb("xb1", [128, D], F32)]
        q_sb = sb("q_sb", [128, 128], BF16)
        qlo = sb("qlo", [1, 128], BF16)
        cs_bf = sb("cs_bf", [128, 2], BF16)
        csT = sb("csT", [2, 128], BF16)
        sum1 = sb("sum1", [128, 32], F32)
        sumsq = sb("sumsq", [128, 32], F32)
        mean = sb("mean", [128, 32], F32)
        msq = sb("msq", [128, 32], F32)
        varv = sb("varv", [128, 32], F32)
        lnv = sb("lnv", [128, 32], F32)
        rstd = sb("rstd", [128, 32], F32)
        nmrs = sb("nmrs", [128, 32], F32)
        maxp = sb("maxp", [128, 10], F32)
        negm = sb("negm", [128, 1], F32)
        Zf = sb("Zf", [128, 1], F32)
        rz = sb("rz", [128, 1], F32)
        spacer = sb("spacer", [128, 64], F32)
        sspace = sb("sspace", [128, 1], F32)
        k0f = sb("k0f", [128, 1], F32)
        k0l = sb("k0l", [1, 1], F32)
        eps_c = sb("eps_c", [128, 1], F32)
        nhalf_c = sb("nhalf_c", [128, 1], F32)

        xn_f32 = scr[:, 0:1024]
        Pf = scr[:, 1024:1536].bitcast(BF16)
        H2 = scr[:, 1536:2048].bitcast(BF16)
        Qf = scr[:, 2048:2560].bitcast(BF16)
        junkS = scr[:, 2560:3072].bitcast(BF16)
        junkV = scr[:, 3072:3584].bitcast(BF16)
        U_bf = P_sb[:, 0:1024]
        UT = P_sb[:, 1024:2048]
        lo_bf = P_sb[0:1, 2048:3072]
        hi_bf = P_sb[:, 0:1024]

        pA = psu("pA", [128, 1024], F32)
        pB = psu("pB", [128, 1024], F32)
        pC = psu("pC", [128, 1024], F32)
        pD = psu("pD", [128, 1024], F32)
        pB_bf = pB[:].bitcast(BF16)
        pC_bf = pC[:].bitcast(BF16)

        P = Prog()
        dX = [Ch("dX0"), Ch("dX1")]
        dW = [Ch("dW0"), Ch("dW1")]
        dWs, dO = Ch("dWs"), Ch("dO")
        V, SC, T = Ch("V"), Ch("S"), Ch("T")
        for c in (dX[0], dX[1], dW[0], dW[1], dWs, dO, V, SC, T):
            c.sem = ctx.enter_context(nc.semaphore(c.name))

        bank_free = {i: None for i in range(8)}
        em = P.emit

        def em2dma(eng, fn, waits=(), inc=None, tag=""):
            return em(eng, fn, waits=waits, inc=inc, tag=tag)

        def vspace(tag="vsp"):
            em("vector", lambda e: e.memset(spacer[:], 0.0), tag=tag)

        em("vector", lambda e: e.memset(sspace[:], 1.0), tag="pre0")
        t_pre = em("vector", lambda e: e.memset(spacer[:], 1.0), inc=V, tag="pre1")
        em("scalar", lambda e: e.activation(sspace[:], sspace[:], AF.Copy),
           waits=[t_pre], inc=SC, tag="actgate")
        em("scalar", lambda e: e.activation(sspace[:], sspace[:], AF.Exp),
           inc=SC, tag="actgate2")
        t_actgate = (SC, SC.n)
        em("vector", lambda e: e.reciprocal(spacer[:, 0:1], spacer[:, 0:1]),
           waits=[t_actgate], inc=V, tag="dvegate")
        t_dvegate = (V, V.n)
        for eng in ("sync", "gpsimd", "tensor"):
            em(eng, lambda e: e.nop(), waits=[t_actgate, t_dvegate], tag="bootw")
        em("gpsimd", lambda e: e.dma_start(out=xb[0][:, 0:512], in_=x_h[0:128, 0:512]),
           inc=dW[0], tag="boot")
        em("gpsimd", lambda e: e.dma_start(out=xb[1][:, 0:512], in_=x_h[0:128, 0:512]),
           inc=dW[1], tag="boot")
        em("gpsimd", lambda e: e.dma_start(out=spacer[:], in_=x_h[0:128, 0:64]),
           inc=dWs, tag="boot")
        em("sync", lambda e: e.dma_start(out=xb[0][:, 512:1024], in_=x_h[0:128, 512:1024]),
           inc=dX[0], tag="boot")
        em("sync", lambda e: e.dma_start(out=xb[1][:, 512:1024], in_=x_h[0:128, 512:1024]),
           inc=dX[1], tag="boot")
        t_bw = [(dW[0], dW[0].n), (dW[1], dW[1].n), (dWs, dWs.n),
                (dX[0], dX[0].n), (dX[1], dX[1].n)]
        em("vector", lambda e: e.reduce_sum(sum1[:, 0:1], xb[0][:], axis=AX.X),
           waits=t_bw, inc=V, tag="bootrd")
        em("vector", lambda e: e.reduce_sum(sum1[:, 1:2], xb[1][:], axis=AX.X),
           inc=V, tag="bootrd")
        t_boot = (V, V.n)

        em2dma("gpsimd", lambda e: e.dma_start(out=ident[:], in_=ident_h.ap()),
           waits=[t_boot], inc=dWs)
        em2dma("gpsimd", lambda e: e.dma_start(
            out=bqT[:], in_=bq_h[0:128].rearrange("(p o) -> p o", o=1),
            allow_slow_non_contiguous=True), inc=dWs)
        em2dma("gpsimd", lambda e: e.dma_start(
            out=bqL[:], in_=bq_h[128:129].rearrange("(p o) -> p o", o=1)), inc=dWs)
        em2dma("gpsimd", lambda e: e.dma_start(
            out=bkT[:], in_=bk_h[0:128].rearrange("(p o) -> p o", o=1),
            allow_slow_non_contiguous=True), inc=dWs)
        em2dma("gpsimd", lambda e: e.dma_start(
            out=bkL[:], in_=bk_h[128:129].rearrange("(p o) -> p o", o=1)), inc=dWs)
        em2dma("gpsimd", lambda e: e.dma_start(
            out=b1T[:], in_=b1_h.rearrange("(c p) -> p c", p=128),
            allow_slow_non_contiguous=True), inc=dWs)
        em2dma("gpsimd", lambda e: e.dma_start(
            out=k0f[:], in_=k0_h[0:128].rearrange("(p o) -> p o", o=1),
            allow_slow_non_contiguous=True), inc=dWs)
        em2dma("gpsimd", lambda e: e.dma_start(
            out=k0l[:], in_=k0_h[128:129].rearrange("(p o) -> p o", o=1)), inc=dWs)
        em2dma("gpsimd", lambda e: e.dma_start(
            out=b2row[0:1, :], in_=b2_h.rearrange("(o d) -> o d", o=1)), inc=dWs)
        em2dma("gpsimd", lambda e: e.dma_start(
            out=bvv0[0:1, :], in_=bv_h.rearrange("(o d) -> o d", o=1)), inc=dWs)
        em2dma("gpsimd", lambda e: e.dma_start(
            out=bvv0[1:2, :], in_=v0_h.rearrange("(o d) -> o d", o=1)), inc=dWs)
        t_smalls = (dWs, dWs.n)
        t_ident = t_smalls
        t_bias = t_smalls
        t_wrow = t_smalls
        em("vector", lambda e: e.memset(ones11[:], 1.0), inc=V)
        em("vector", lambda e: e.memset(onescol[:], 1.0), inc=V)
        em("vector", lambda e: e.memset(onesc1[:], 1.0), inc=V)
        t_ones = (V, V.n)
        em("vector", lambda e: e.memset(eps_c[:], EPS), inc=V)
        em("vector", lambda e: e.memset(nhalf_c[:], -0.5), inc=V)
        t_consts = (V, V.n)
        em("vector", lambda e: e.tensor_copy(K_hi[:, 4096:4097], k0f[:]),
           waits=[t_smalls], inc=V)
        em("vector", lambda e: e.tensor_copy(K_lo[0:1, 4096:4097], k0l[:]), inc=V)
        t_k0 = (V, V.n)

        stage_tok = [t_boot, t_boot]
        load_tok = [None, None]
        stage_i = [0]

        def stage_load(src_ap, rows=128, cols=D):
            i = stage_i[0] & 1
            stage_i[0] += 1
            dst = xb[i][0:rows, 0:cols]
            tk = em2dma("gpsimd", lambda e, d=dst, s=src_ap: e.dma_start(out=d, in_=s),
                    waits=[stage_tok[i]], inc=dW[i], tag="wload")
            load_tok[i] = tk
            return i, dst

        def load_cast(src_ap, dst_bf, rows=128, cols=D, extra=None):
            i, st = stage_load(src_ap, rows, cols)
            tk = em("vector", lambda e, d=dst_bf, s=st: e.tensor_copy(d, s),
                    waits=[load_tok[i], extra], inc=V, tag="wcast")
            stage_tok[i] = tk
            return tk

        for c in range(8):
            load_cast(U1_h[c * 128:(c + 1) * 128, :], U1s[:, c * 256:(c + 1) * 256], cols=256)
        for m in range(2):
            for pce in range(4):
                load_cast(V1_h[m * 128:(m + 1) * 128, pce * 1024:(pce + 1) * 1024],
                          V1s[:, m * 4096 + pce * 1024: m * 4096 + (pce + 1) * 1024])
        for c in range(32):
            load_cast(U2_h[c * 128:(c + 1) * 128, :], U2s[:, c * 256:(c + 1) * 256], cols=256)
        for m in range(2):
            load_cast(V2_h[m * 128:(m + 1) * 128, :], V2s[:, m * D:(m + 1) * D])

        def prep_qk(w_h, wT):
            tcast = load_cast(w_h[0:128, :], hi_bf, extra=bank_free[4])
            for c in range(8):
                em("tensor", lambda e, c=c: e.transpose(
                    pC_bf[:, c * 128:(c + 1) * 128], hi_bf[:, c * 128:(c + 1) * 128], ident[:]),
                    waits=[tcast, t_ident, bank_free[4] if c == 0 else None],
                    inc=(T, 1) if c == 7 else None, tag="qkT")
            t_tr = (T, T.n)
            wT3 = wT[:].rearrange("p (c k) -> p c k", k=129)
            pc3 = pC_bf[:, 0:1024].rearrange("p (c k) -> p c k", k=128)
            ev = em("vector", lambda e, w3=wT3, p3=pc3: e.tensor_copy(w3[:, :, 0:128], p3),
                    waits=[t_tr], inc=V, tag="qkTev")
            bank_free[4] = ev
            tc2 = load_cast(w_h[128:129, :], lo_bf[0:1, :], rows=1, extra=bank_free[6])
            for c in range(8):
                em("tensor", lambda e, c=c: e.matmul(
                    pD[:, c:c + 1], lo_bf[0:1, c * 128:(c + 1) * 128], ones11[:],
                    start=True, stop=True),
                    waits=[tc2, t_ones, bank_free[6] if c == 0 else None],
                    inc=(T, 1) if c == 7 else None, tag="qklo")
            t_lo = (T, T.n)
            ev2 = em("vector", lambda e, w3=wT3: e.tensor_copy(
                w3[:, :, 128:129], pD[:, 0:8].rearrange("p (c k) -> p c k", k=1)),
                waits=[t_lo], inc=V, tag="qkloev")
            bank_free[6] = ev2

        prep_qk(wq_h, wqT)
        prep_qk(wk_h, wkT)

        wv3 = wvT[:].rearrange("p (dc e) -> p dc e", e=1024)
        for ec in range(8):
            tcast = load_cast(wv_h[ec * 128:(ec + 1) * 128, :], hi_bf, extra=bank_free[4])
            for c in range(8):
                em("tensor", lambda e, c=c: e.transpose(
                    pC_bf[:, c * 128:(c + 1) * 128], hi_bf[:, c * 128:(c + 1) * 128], ident[:]),
                    waits=[tcast, bank_free[4] if c == 0 else None],
                    inc=(T, 1) if c == 7 else None, tag="wvT")
            t_tr = (T, T.n)
            ev = em("vector", lambda e, ec=ec: e.tensor_copy(
                wv3[:, :, ec * 128:(ec + 1) * 128],
                pC_bf[:, 0:1024].rearrange("p (c k) -> p c k", k=128)),
                waits=[t_tr], inc=V, tag="wvTev")
            bank_free[4] = ev

        def ln_stats(x_t, waits, t):
            em("vector", lambda e, t=t: e.scalar_tensor_tensor(
                junkV, x_t, 1.0, x_t, op0=ALU.mult, op1=ALU.mult,
                accum_out=sumsq[:, t:t + 1]),
                waits=waits, inc=V, tag="lnsq")
            tk = em("vector", lambda e, t=t: e.reduce_sum(
                sum1[:, t:t + 1], x_t, axis=AX.X), inc=V, tag="lnsum")
            return tk

        def vgap(n=16):
            for _ in range(n):
                em("vector", lambda e: e.memset(junkV, 0.0), tag="vgap")

        def sgap(n=5):
            for _ in range(n):
                em("scalar", lambda e: e.activation(sspace[:], sspace[:], AF.Copy),
                   tag="sgap")

        def ln_chain_batched(t_stats_done):
            em("vector", lambda e: e.tensor_scalar_mul(mean[:], sum1[:], 1.0 / D),
               waits=[t_stats_done], tag="lnmean")
            em("vector", lambda e: e.tensor_scalar_mul(msq[:], sumsq[:], 1.0 / D),
               tag="lnmsq")
            vgap()
            em("vector", lambda e: e.tensor_mul(varv[:], mean[:], mean[:]), tag="lnm2")
            vgap()
            tv2 = em("vector", lambda e: e.tensor_sub(varv[:], msq[:], varv[:]),
                     inc=V, tag="lnvar")
            sgap()
            em("scalar", lambda e: e.activation(lnv[:], varv[:], AF.Ln, bias=eps_c[:]),
               waits=[tv2, t_consts], tag="lnln")
            sgap()
            em("scalar", lambda e: e.activation(rstd[:], lnv[:], AF.Exp, scale=nhalf_c[:]),
               tag="lnexp")
            tr = em("scalar", lambda e: e.activation(sspace[:], sspace[:], AF.Copy),
                    inc=SC, tag="lnexp_m")
            vgap()
            tn = em("vector", lambda e: e.scalar_tensor_tensor(
                nmrs[:], mean[:], -1.0, rstd[:], op0=ALU.mult, op1=ALU.mult),
                waits=[tr], inc=V, tag="lnnm")
            tnm = em("vector", lambda e: e.memset(junkV, 0.0), inc=V, tag="lnnm_m")
            return tnm

        def ln_affine(x_t, waits, t, out_ap):
            return em("scalar", lambda e, o=out_ap, t=t: e.activation(
                o, x_t, AF.Identity, bias=nmrs[:, t:t + 1], scale=rstd[:, t:t + 1]),
                waits=waits, inc=SC, tag="lnaff")

        def xnt_group(tiles, src_toks):
            for c in range(8):
                bk = 4 + (c & 1)
                off = (c & 1) * 1024
                for j, tl in enumerate(tiles):
                    em("tensor", lambda e, c=c, tl=tl, off=off, j=j: e.transpose(
                        pC_bf[:, off + j * 128: off + (j + 1) * 128],
                        XN[:, tl * D + c * 128: tl * D + (c + 1) * 128], ident[:]),
                        waits=[src_toks[j], bank_free[bk] if j == 0 else None],
                        inc=(T, 1) if j == 3 else None, tag="xnT")
                t_tr = (T, T.n)
                ev = em("vector", lambda e, c=c, off=off: e.tensor_copy(
                    xnT[:, c * 512:(c + 1) * 512], pC_bf[:, off:off + 512]),
                    waits=[t_tr], inc=V, tag="xnTev")
                bank_free[bk] = ev
            return (V, V.n)

        xb_tok = [stage_tok[0], stage_tok[1]]
        xn_tok = [None] * NT1
        xload_tok = [None] * (NT1 + 1)
        for t in range(NT1):
            i = t & 1
            xload_tok[t] = em2dma("sync", lambda e, t=t, i=i: e.dma_start(
                out=xb[i][:], in_=x_h[t * 128:(t + 1) * 128, :]),
                waits=[xb_tok[i]], inc=dX[i], tag="xloadA")
            if t > 0:
                xb_tok[(t - 1) & 1] = ln_stats(
                    xb[(t - 1) & 1][:], [xload_tok[t - 1], xload_tok[t]], t - 1)
        xload_tok[NT1] = em2dma("sync", lambda e: e.dma_start(
            out=xb[0][:], in_=x_h[0:128, :]), waits=[xb_tok[0]], inc=dX[0],
            tag="xloadAd")
        xb_tok[0] = xload_tok[NT1]
        xb_tok[1] = ln_stats(xb[1][:], [xload_tok[NT1 - 1], xload_tok[NT1]], NT1 - 1)
        t_chain = ln_chain_batched(xb_tok[1])
        def kproj_group(g):
            t_xnt = xnt_group([g * 4 + t for t in range(4)],
                              [xn_tok[g * 4 + t] for t in range(4)])
            for c in range(8):
                em("tensor", lambda e, c=c: e.matmul(
                    pA[:, 0:512], wkT[:, c * 129:c * 129 + 128],
                    xnT[:, c * 512:(c + 1) * 512],
                    start=(c == 0), stop=(c == 7)),
                    waits=[t_xnt, bank_free[0] if c == 0 else None], tag="kprojhi")
                em("tensor", lambda e, c=c: e.matmul(
                    pA[0:1, 512:1024], wkT[:, c * 129 + 128:c * 129 + 129],
                    xnT[:, c * 512:(c + 1) * 512],
                    start=(c == 0), stop=(c == 7)),
                    waits=[bank_free[1] if c == 0 else None],
                    inc=(T, 1) if c == 7 else None, tag="kprojlo")
            t_k = (T, T.n)
            e1 = em("scalar", lambda e, g=g: e.activation(
                K_hi[:, g * 512:(g + 1) * 512], pA[:, 0:512], AF.Identity, bias=bkT[:]),
                waits=[t_k, t_bias], inc=SC, tag="kevhi")
            e2 = em("scalar", lambda e, g=g: e.activation(
                K_lo[0:1, g * 512:(g + 1) * 512], pA[0:1, 512:1024], AF.Identity,
                bias=bkL[:]),
                inc=SC, tag="kevlo")
            bank_free[0] = e1
            bank_free[1] = e2

        for idx in range(NT1 + 1):
            i = idx & 1
            tsrc = 0 if idx == NT1 else idx
            xload_tok[idx] = em2dma("sync", lambda e, tsrc=tsrc, i=i: e.dma_start(
                out=xb[i][:], in_=x_h[tsrc * 128:(tsrc + 1) * 128, :]),
                waits=[xb_tok[i]], inc=dX[i], tag="xloadB")
            if idx == NT1:
                xb_tok[0] = xload_tok[idx]
            if idx > 0:
                tile = idx - 1
                pt = tile & 1
                ta = ln_affine(xb[pt][:], [xload_tok[tile], xload_tok[idx], t_chain],
                               tile, XN[:, tile * D:(tile + 1) * D])
                xb_tok[pt] = ta
                xn_tok[tile] = ta
            if idx > 0 and idx % 4 == 0:
                kproj_group(idx // 4 - 1)
        t_xn_done = xn_tok[NT1 - 1]
        t_k_done = (SC, SC.n)

        acc_free = None
        last_exp = [None]
        prev_tadd = [None]
        for g in range(NG2):
            acc_toks = [None] * 4
            for t in range(4):
                tile = g * 4 + t
                i = tile & 1
                xload_tok[tile] = em2dma("sync", lambda e, tile=tile, i=i: e.dma_start(
                    out=xb[i][:], in_=x_h[tile * 128:(tile + 1) * 128, :]),
                    waits=[xb_tok[i]], inc=dX[i], tag="xloadC")
                if t > 0:
                    tile_p = tile - 1
                    pt = tile_p & 1
                    ta = ln_affine(xb[pt][:],
                                   [xload_tok[tile_p], xload_tok[tile], t_chain,
                                    prev_tadd[0]],
                                   tile_p, xn_f32)
                    xn_tok[tile_p] = ta
                    tadd = em("vector", lambda e, t=t, pt=pt: e.tensor_add(
                        acc[:, (t - 1) * D:t * D], xb[pt][:], xn_f32),
                        waits=[ta, acc_free], inc=V, tag="accinit")
                    xb_tok[pt] = tadd
                    acc_toks[t - 1] = tadd
                    prev_tadd[0] = tadd
            tile_p = g * 4 + 3
            pt = tile_p & 1
            idum = 1 - pt
            tdum = em2dma("sync", lambda e, idum=idum: e.dma_start(
                out=xb[idum][:], in_=x_h[0:128, :]),
                waits=[xb_tok[idum]], inc=dX[idum], tag="xloadCd")
            xb_tok[idum] = tdum
            ta = ln_affine(xb[pt][:], [xload_tok[tile_p], tdum, t_chain,
                                       prev_tadd[0]],
                           tile_p, xn_f32)
            xn_tok[tile_p] = ta
            tadd = em("vector", lambda e, pt=pt: e.tensor_add(
                acc[:, 3 * D:4 * D], xb[pt][:], xn_f32),
                waits=[ta, acc_free], inc=V, tag="accinit")
            xb_tok[pt] = tadd
            acc_toks[3] = tadd
            prev_tadd[0] = tadd
            t_xnt = xnt_group([g * 4 + t for t in range(4)],
                              [xn_tok[g * 4 + t] for t in range(4)])
            for c in range(8):
                for m in range(2):
                    em("tensor", lambda e, c=c, m=m: e.matmul(
                        pA[:, m * 512:(m + 1) * 512],
                        U1s[:, c * 256 + m * 128: c * 256 + (m + 1) * 128],
                        xnT[:, c * 512:(c + 1) * 512], start=(c == 0), stop=(c == 7)),
                        waits=[t_xnt if c == 0 else None,
                               bank_free[m] if c == 0 else None],
                        inc=(T, 1) if (c == 7 and m == 1) else None, tag="ffnP")
            t_p = (T, T.n)
            ep = em("scalar", lambda e: e.activation(Pf, pA[:], AF.Copy),
                    waits=[t_p], inc=SC, tag="ffnPev")
            bank_free[0] = ep
            bank_free[1] = ep
            gelu_tok = [None, None]
            for j in range(32):
                jb = j & 1
                em("tensor", lambda e, j=j, jb=jb: e.matmul(
                    pB[:, jb * 512:(jb + 1) * 512],
                    V1s[:, 0 * 4096 + j * 128:0 * 4096 + (j + 1) * 128],
                    Pf[:, 0:512], start=True, stop=False),
                    waits=[ep if j < 2 else None,
                           (gelu_tok[jb] if j >= 2 else bank_free[2 + jb])], tag="ffnH")
                em("tensor", lambda e, j=j, jb=jb: e.matmul(
                    pB[:, jb * 512:(jb + 1) * 512],
                    V1s[:, 1 * 4096 + j * 128:1 * 4096 + (j + 1) * 128],
                    Pf[:, 512:1024], start=False, stop=True),
                    inc=(T, 1), tag="ffnH2")
                t_h = (T, T.n)
                gt = em("scalar", lambda e, j=j, jb=jb: e.activation(
                    H2[:, jb * 512:(jb + 1) * 512], pB[:, jb * 512:(jb + 1) * 512],
                    AF.Gelu_apprx_tanh, bias=b1T[:, j:j + 1]),
                    waits=[t_h, t_smalls], inc=SC, tag="gelu")
                gelu_tok[jb] = gt
                for m in range(2):
                    em("tensor", lambda e, j=j, jb=jb, m=m: e.matmul(
                        pC[:, m * 512:(m + 1) * 512],
                        U2s[:, j * 256 + m * 128:j * 256 + (m + 1) * 128],
                        H2[:, jb * 512:(jb + 1) * 512], start=(j == 0), stop=(j == 31)),
                        waits=[gt, (bank_free[4 + m] if j == 0 else None)],
                        inc=(T, 1) if (j == 31 and m == 1) else None, tag="ffnQ")
            bank_free[2] = gelu_tok[0]
            bank_free[3] = gelu_tok[1]
            t_q = (T, T.n)
            eq = em("vector", lambda e: e.tensor_copy(Qf, pC[:]),
                    waits=[t_q], inc=V, tag="ffnQev")
            bank_free[4] = eq
            bank_free[5] = eq
            for t in range(4):
                for nh in range(2):
                    for m in range(2):
                        em("tensor", lambda e, t=t, nh=nh, m=m: e.matmul(
                            pD[:, nh * 512:(nh + 1) * 512],
                            Qf[:, m * 512 + t * 128: m * 512 + (t + 1) * 128],
                            V2s[:, m * D + nh * 512: m * D + (nh + 1) * 512],
                            start=(m == 0), stop=False),
                            waits=[eq if m == 0 else None,
                                   bank_free[6 + nh] if m == 0 else None], tag="ffnO")
                    em("tensor", lambda e, nh=nh: e.matmul(
                        pD[:, nh * 512:(nh + 1) * 512], onescol[:],
                        b2row[0:1, nh * 512:(nh + 1) * 512], start=False, stop=True),
                        waits=[t_ones, t_wrow],
                        inc=(T, 1) if nh == 1 else None, tag="ffnOb")
                t_o = (T, T.n)
                eo = em("vector", lambda e, t=t: e.tensor_add(
                    acc[:, t * D:(t + 1) * D], acc[:, t * D:(t + 1) * D], pD[:]),
                    waits=[t_o, acc_toks[t]], inc=V, tag="accO")
                bank_free[6] = eo
                bank_free[7] = eo
            def emit_qproj(t, w0, w1):
                for c in range(8):
                    em("tensor", lambda e, c=c, t=t: e.matmul(
                        pA[:, 0:128], wqT[:, c * 129:c * 129 + 128],
                        xnT[:, c * 512 + t * 128: c * 512 + (t + 1) * 128],
                        start=(c == 0), stop=(c == 7)),
                        waits=[w0 if c == 0 else None], tag="qhi")
                    em("tensor", lambda e, c=c, t=t: e.matmul(
                        pA[0:1, 512:640], wqT[:, c * 129 + 128:c * 129 + 129],
                        xnT[:, c * 512 + t * 128: c * 512 + (t + 1) * 128],
                        start=(c == 0), stop=(c == 7)),
                        waits=[w1 if c == 0 else None],
                        inc=(T, 1) if c == 7 else None, tag="qlo")
                return (T, T.n)

            def emit_qev(t_qp):
                eqh = em("scalar", lambda e: e.activation(
                    q_sb[:], pA[:, 0:128], AF.Identity, bias=bqT[:]),
                    waits=[t_qp, t_bias], inc=SC, tag="qevhi")
                eql = em("scalar", lambda e: e.activation(
                    qlo[:], pA[0:1, 512:640], AF.Identity, bias=bqL[:]),
                    inc=SC, tag="qevlo")
                return eqh, eql

            pend = None
            for t in range(4):
                if pend is None:
                    eqh, eql = emit_qev(emit_qproj(t, bank_free[0], bank_free[1]))
                    bank_free[0] = eqh
                else:
                    eqh, eql = pend
                slab_ev = [None] * 8
                for s_ in range(8):
                    sb_ = s_ & 1
                    em("tensor", lambda e, s_=s_, sb_=sb_: e.matmul(
                        pB[:, sb_ * 512:(sb_ + 1) * 512], q_sb[:],
                        K_hi[:, s_ * 512:(s_ + 1) * 512],
                        start=True, stop=False),
                        waits=[eqh,
                               t_k_done if s_ == 0 else None,
                               bank_free[0] if s_ == 0 else None,
                               (slab_ev[s_ - 2] if s_ >= 2 else bank_free[2 + sb_])],
                        tag="schi")
                    em("tensor", lambda e, s_=s_, sb_=sb_: e.matmul(
                        pB[:, sb_ * 512:(sb_ + 1) * 512], qlo[:],
                        K_lo[0:1, s_ * 512:(s_ + 1) * 512],
                        start=False, stop=True),
                        waits=[eql], inc=(T, 1), tag="sclo")
                    t_sc = (T, T.n)
                    se = em("scalar", lambda e, s_=s_, sb_=sb_: e.activation(
                        scr[:, s_ * 512:(s_ + 1) * 512], pB[:, sb_ * 512:(sb_ + 1) * 512],
                        AF.Copy),
                        waits=[t_sc, acc_toks[3] if (t == 0 and s_ == 0) else None],
                        inc=SC, tag="scev")
                    slab_ev[s_] = se
                    em("vector", lambda e, s_=s_: e.reduce_max(
                        maxp[:, s_:s_ + 1], scr[:, s_ * 512:(s_ + 1) * 512], axis=AX.X),
                        waits=[se], tag="maxp")
                bank_free[2] = slab_ev[6]
                bank_free[3] = slab_ev[7]
                em("tensor", lambda e: e.matmul(
                    pA[:, 640:641], q_sb[:], K_hi[:, 4096:4097], start=True, stop=False),
                    waits=[eql, t_k0], tag="snkhi")
                em("tensor", lambda e: e.matmul(
                    pA[:, 640:641], qlo[:], K_lo[0:1, 4096:4097], start=False, stop=True),
                    inc=(T, 1), tag="snklo")
                t_snk = (T, T.n)
                es = em("scalar", lambda e: e.activation(
                    scr[:, 4096:4097], pA[:, 640:641], AF.Copy),
                    waits=[t_snk], inc=SC, tag="snkev")
                bank_free[1] = es
                em("vector", lambda e: e.tensor_copy(maxp[:, 8:9], scr[:, 4096:4097]),
                   waits=[es], tag="snkmax")
                if t < 3:
                    t_qp_n = emit_qproj(t + 1, bank_free[0], es)
                vspace("vsp_negm")
                tm = em("vector", lambda e: e.reduce_max(
                    negm[:], maxp[:, 0:9], axis=AX.X, negate=True), inc=V, tag="negm")
                te = em("scalar", lambda e: e.activation(
                    P_sb[:, 0:SK], scr[:, 0:SK], AF.Exp, bias=negm[:]),
                    waits=[tm], inc=SC, tag="exp")
                last_exp[0] = te
                if t < 3:
                    pend = emit_qev(t_qp_n)
                    y_w0, y_w1 = pend
                else:
                    pend = None
                    y_w0, y_w1 = eqh, es
                pt_ev = [None] * 8
                attnv_tok = [None] * 8
                for k in range(8):
                    bk = 4 + (k & 1)
                    off = (k & 1) * 1024
                    for j in range(4):
                        em("tensor", lambda e, k=k, j=j, off=off: e.transpose(
                            pC_bf[:, off + j * 128: off + (j + 1) * 128],
                            P_sb[:, (4 * k + j) * 128: (4 * k + j + 1) * 128], ident[:]),
                            waits=[te if (k == 0 and j == 0) else None,
                                   ((bank_free[bk] if k < 2 else pt_ev[k - 2])
                                    if j == 0 else None)],
                            inc=(T, 1) if j == 3 else None, tag="ptT")
                    t_pt = (T, T.n)
                    pe = em("vector", lambda e, k=k, off=off: e.tensor_copy(
                        PT[:, (k & 3) * 512:((k & 3) + 1) * 512], pC_bf[:, off:off + 512]),
                        waits=[t_pt, attnv_tok[k - 4] if k >= 4 else None],
                        inc=V, tag="ptev")
                    pt_ev[k] = pe
                    for cc in range(4):
                        c = 4 * k + cc
                        for nh in range(2):
                            em("tensor", lambda e, k=k, cc=cc, c=c, nh=nh: e.matmul(
                                pD[:, nh * 512:(nh + 1) * 512],
                                PT[:, (k & 3) * 512 + cc * 128: (k & 3) * 512 + (cc + 1) * 128],
                                XN[:, c * D + nh * 512: c * D + (nh + 1) * 512],
                                start=(c == 0), stop=(c == 31)),
                                waits=[pe if (cc == 0 and nh == 0) else None,
                                       t_xn_done if c == 0 else None,
                                       bank_free[6 + nh] if c == 0 else None],
                                tag="attnV")
                        em("tensor", lambda e, k=k, cc=cc, c=c: e.matmul(
                            pB[:, 512:513],
                            PT[:, (k & 3) * 512 + cc * 128: (k & 3) * 512 + (cc + 1) * 128],
                            onesc1[:], start=(c == 0), stop=(c == 31)),
                            waits=[(t_ones if c == 0 else None),
                                   (bank_free[3] if c == 0 else None)],
                            inc=(T, 1) if cc == 3 else None, tag="zr")
                    attnv_tok[k] = (T, T.n)
                t_u = attnv_tok[7]
                eu = em("vector", lambda e: e.tensor_copy(U_bf, pD[:]),
                        waits=[t_u], inc=V, tag="uev")
                bank_free[6] = eu
                bank_free[7] = eu
                em("vector", lambda e: e.tensor_copy(cs_bf[:, 0:1], pB[:, 512:513]),
                   waits=[t_u], tag="cs0")
                tcs = em("vector", lambda e: e.tensor_copy(cs_bf[:, 1:2], P_sb[:, 4096:4097]),
                         inc=V, tag="cs1")
                tct = em("tensor", lambda e: e.transpose(pB_bf[0:2, 0:128], cs_bf[:], ident[:]),
                         waits=[tcs, bank_free[2]], inc=(T, 1), tag="csT")
                tz = em("vector", lambda e: e.tensor_add(
                    Zf[:], pB[:, 512:513], P_sb[:, 4096:4097]), inc=V, tag="zadd")
                bank_free[3] = tz
                ecs = em("vector", lambda e: e.tensor_copy(csT[:], pB_bf[0:2, 0:128]),
                         waits=[tct], inc=V, tag="csTev")
                bank_free[2] = ecs
                em("vector", lambda e: e.reciprocal(rz[:], Zf[:]), tag="rz")
                ut_ev = [None, None]
                for kb in range(2):
                    off = kb * 1024
                    for j in range(4):
                        em("tensor", lambda e, kb=kb, j=j, off=off: e.transpose(
                            pC_bf[:, off + j * 128: off + (j + 1) * 128],
                            U_bf[:, (4 * kb + j) * 128: (4 * kb + j + 1) * 128], ident[:]),
                            waits=[eu if j == 0 else None,
                                   pt_ev[6 + kb] if j == 0 else None],
                            inc=(T, 1) if j == 3 else None, tag="utT")
                    t_ut = (T, T.n)
                    ue = em("vector", lambda e, kb=kb, off=off: e.tensor_copy(
                        UT[:, kb * 512:(kb + 1) * 512], pC_bf[:, off:off + 512]),
                        waits=[t_ut], inc=V, tag="utev")
                    ut_ev[kb] = ue
                bank_free[4] = ut_ev[0]
                bank_free[5] = ut_ev[1]
                for c in range(8):
                    for nh in range(2):
                        em("tensor", lambda e, c=c, nh=nh: e.matmul(
                            pA[:, nh * 512:(nh + 1) * 512], UT[:, c * 128:(c + 1) * 128],
                            wvT[:, c * D + nh * 512: c * D + (nh + 1) * 512],
                            start=(c == 0), stop=False),
                            waits=[ut_ev[c // 4] if nh == 0 else None,
                                   (y_w0 if nh == 0 else y_w1) if c == 0 else None,
                                   ecs if c == 0 else None], tag="Y")
                for nh in range(2):
                    em("tensor", lambda e, nh=nh: e.matmul(
                        pA[:, nh * 512:(nh + 1) * 512], csT[:],
                        bvv0[:, nh * 512:(nh + 1) * 512], start=False, stop=True),
                        waits=[ecs, t_wrow], inc=(T, 1) if nh == 1 else None, tag="Yr2")
                t_y = (T, T.n)
                ey = em("vector", lambda e, t=t: e.scalar_tensor_tensor(
                    acc[:, t * D:(t + 1) * D], pA[:], rz[:], acc[:, t * D:(t + 1) * D],
                    op0=ALU.mult, op1=ALU.add),
                    waits=[t_y], inc=V, tag="accY")
                bank_free[0] = ey
                bank_free[1] = ey
            so = em("gpsimd", lambda e, g=g: e.dma_start(
                out=out_h[g * 512:(g + 1) * 512, :].rearrange("(t p) d -> p t d", p=128),
                in_=acc[:].rearrange("p (t d) -> p t d", t=4)),
                waits=[(V, V.n)], inc=dO, tag="store")
            acc_free = so
        em("gpsimd", lambda e: e.engine_nop(), waits=[(dO, dO.n)], tag="fin")

        P.check()
        with nc.Block() as block:
            P.realize(block)
    return nc


_NC_CACHE = [None]


def kernel(L0_attn_x, L0_attn_w_q, L0_attn_b_q, L0_attn_w_k, L0_attn_b_k,
           L0_attn_w_v, L0_attn_b_v, L0_attn_k_0, L0_attn_v_0,
           L0_ffn_U1, L0_ffn_V1, L0_ffn_b1, L0_ffn_U2, L0_ffn_V2, L0_ffn_b2):
    x = np.ascontiguousarray(np.asarray(L0_attn_x, dtype=np.float32))
    w = {
        "wq": np.ascontiguousarray(L0_attn_w_q, dtype=np.float32),
        "bq": np.ascontiguousarray(L0_attn_b_q, dtype=np.float32),
        "wk": np.ascontiguousarray(L0_attn_w_k, dtype=np.float32),
        "bk": np.ascontiguousarray(L0_attn_b_k, dtype=np.float32),
        "wv": np.ascontiguousarray(L0_attn_w_v, dtype=np.float32),
        "bv": np.ascontiguousarray(L0_attn_b_v, dtype=np.float32),
        "k0": np.ascontiguousarray(L0_attn_k_0, dtype=np.float32),
        "v0": np.ascontiguousarray(L0_attn_v_0, dtype=np.float32),
        "U1": np.ascontiguousarray(L0_ffn_U1, dtype=np.float32),
        "V1": np.ascontiguousarray(L0_ffn_V1, dtype=np.float32),
        "b1": np.ascontiguousarray(L0_ffn_b1, dtype=np.float32),
        "U2": np.ascontiguousarray(L0_ffn_U2, dtype=np.float32),
        "V2": np.ascontiguousarray(L0_ffn_V2, dtype=np.float32),
        "b2": np.ascontiguousarray(L0_ffn_b2, dtype=np.float32),
    }
    if _NC_CACHE[0] is None:
        _NC_CACHE[0] = build_nc()
    nc = _NC_CACHE[0]
    in_maps = []
    for core in range(8):
        b, h = core // 2, core % 2
        xb_ = x[b]
        xr = np.concatenate([xb_[h * QR:(h + 1) * QR], xb_[(1 - h) * QR:(2 - h) * QR]], axis=0)
        m = {"x": np.ascontiguousarray(xr)}
        m.update(w)
        in_maps.append(m)
    res = run_bass_kernel_spmd(nc, in_maps, core_ids=list(range(8)))
    out = np.empty((B, S, D), dtype=np.float32)
    for core in range(8):
        b, h = core // 2, core % 2
        out[b, h * QR:(h + 1) * QR] = res.results[core]["out"]
    return out
